# version 18
# baseline (speedup 1.0000x reference)
import numpy as np
import ml_dtypes
import concourse.bass as bass
import concourse.mybir as mybir
from concourse.bass_utils import run_bass_kernel_spmd
from concourse.bass_core_rust import IndirectOffsetOnAxis

F32 = mybir.dt.float32
BF16 = mybir.dt.bfloat16
AF = mybir.ActivationFunctionType
ALU = mybir.AluOpType
AX = mybir.AxisListType
BF = ml_dtypes.bfloat16

B, S, E = 2, 4096, 2048
HQ, KVH, D = 16, 4, 128
HG, RNK, SYM = 12, 32, 4
FF, NEXP = 1408, 8
NKT = S // 128
NOT = 8
NKC = 16
EXP_SHIFT = -12.0
RMS_EPS = 1e-6
LN_EPS = 1e-5
FCH = FF // 128
I32 = mybir.dt.int32
CAP = 384
NSLT = CAP // 128
BIGV = 100000.0
DUMPT = 1024


def _flat(waits):
    out = []
    for t in waits:
        if t is None:
            continue
        if isinstance(t, list):
            out.extend(x for x in t if x is not None)
        else:
            out.append(t)
    return out


class Prog:
    ENGS = ("te", "ve", "se", "gp", "sy")
    NPOOL = {"sy": 12, "gp": 8}

    def __init__(self, nc):
        self.nc = nc
        self.sem = {}
        self.cnt = {}
        for n in ("te", "ve", "se"):
            self.sem[n] = nc.alloc_semaphore("s_" + n)
            self.cnt[n] = 0
        self.pool = {}
        for q, k in self.NPOOL.items():
            self.pool[q] = []
            for i in range(k):
                nm = f"{q}{i}"
                self.sem[nm] = nc.alloc_semaphore("s_" + nm)
                self.cnt[nm] = 0
                self.pool[q].append(nm)
        self.rr = {q: 0 for q in self.NPOOL}
        self.phases = []
        self.cur = None
        self.waited = None

    def phase(self, name=""):
        self.cur = {e: [] for e in self.ENGS}
        self.waited = {e: {} for e in self.ENGS}
        self.phases.append(self.cur)

    def op(self, eng, fn, waits=(), inc=True):
        w = []
        for (s, v) in _flat(waits):
            if self.waited[eng].get(s, 0) < v:
                self.waited[eng][s] = v
                w.append((s, v))
        tok = None
        if eng in ("sy", "gp"):
            if inc:
                nm = self.pool[eng][self.rr[eng] % len(self.pool[eng])]
                self.rr[eng] += 1
                if self.cnt[nm] > 0 and self.waited[eng].get(nm, 0) < self.cnt[nm]:
                    w.append((nm, self.cnt[nm]))
                    self.waited[eng][nm] = self.cnt[nm]
                self.cnt[nm] += 16
                tok = (nm, self.cnt[nm])
                self.cur[eng].append((w, fn, nm, 16))
            else:
                self.cur[eng].append((w, fn, None, 16))
            return tok
        if inc:
            self.cnt[eng] += 1
            tok = (eng, self.cnt[eng])
        self.cur[eng].append((w, fn, eng if inc else None, 1))
        return tok

    def tok(self, eng):
        return (eng, self.cnt[eng]) if self.cnt.get(eng) else None

    def emit(self):
        nc = self.nc
        for ph in self.phases:
            if not any(ops for ops in ph.values()):
                continue
            with nc.Block() as blk:
                for ename in self.ENGS:
                    ops = ph[ename]
                    if not ops:
                        continue
                    sec = {"te": blk.tensor, "ve": blk.vector, "se": blk.scalar,
                           "gp": blk.gpsimd, "sy": blk.sync}[ename]

                    def body(e, ops=ops, ename=ename):
                        for (w, fn, inc, n) in ops:
                            for (s, v) in w:
                                e.wait_ge(self.sem[s], v)
                            inst = fn(e)
                            if inc is not None:
                                inst.then_inc(self.sem[inc], n)
                    sec(body)


def _host_tables(j):
    inv_freq = 1.0 / 10000.0 ** (np.arange(0, D, 2, dtype=np.float64) / D)
    tpos = np.arange(S, dtype=np.float64)
    freqs = np.outer(tpos, inv_freq)
    emb = np.concatenate([freqs, freqs], -1)
    cos = np.cos(emb).astype(np.float32)
    sin = np.sin(emb).astype(np.float32)
    cos_k = np.ascontiguousarray(cos.reshape(NKT, 128, D))
    sin_k = np.ascontiguousarray(sin.reshape(NKT, 128, D))
    own = 4 * np.arange(1024) + j
    cos_q = np.ascontiguousarray(cos[own].reshape(NOT, 128, D))
    sin_q = np.ascontiguousarray(sin[own].reshape(NOT, 128, D))
    t = np.arange(128)[:, None]
    s4 = 4 * np.arange(512)[None, :]
    masks = np.ascontiguousarray(
        np.stack([(t - s4 <= j - 128 * d) for d in range(16)]).astype(BF))
    return cos_k, sin_k, cos_q, sin_q, masks


def build_program():
    nc = bass.Bass("TRN2", target_bir_lowering=False)
    pr = Prog(nc)

    def din(name, shape, dtype=F32):
        return nc.dram_tensor(name, shape, dtype, kind="ExternalInput").ap()

    x_own = din("x_own", [1024, E])
    x_keys = din("x_keys", [S, E])
    Wq = din("Wq", [E, E])
    Wk = din("Wk", [E, KVH * D])
    Wv = din("Wv", [E, KVH * D])
    Wo = din("Wo", [E, E])
    Wrq = din("Wrq", [E, HG * RNK])
    Wrk = din("Wrk", [E, HG * RNK])
    qnw = din("qnw", [D])
    knw = din("knw", [D])
    gsc = din("gsc", [HG])
    gate_w = din("gate_w", [E, NEXP])
    w_gate = din("w_gate", [NEXP, E, FF])
    w_up = din("w_up", [NEXP, E, FF])
    w_down = din("w_down", [NEXP, FF, E])
    ideye = din("ideye", [128, 128], BF16)
    masks_in = din("masks_in", [16, 128, 512], BF16)
    cosk_in = din("cos_k", [NKT, 128, D])
    sink_in = din("sin_k", [NKT, 128, D])
    cosq_in = din("cos_q", [NOT, 128, D])
    sinq_in = din("sin_q", [NOT, 128, D])
    moe_init = din("moe_init", [NEXP, CAP + 16, 2])
    erow_in = din("erow", [NEXP])
    tokid_in = din("tokid_c", [1024])
    out_own = nc.dram_tensor("out_own", [1024, E], F32, kind="ExternalOutput").ap()
    aod_hi = nc.dram_tensor("aod_hi", [NOT, NKC, 128, 128], BF16).ap()
    aod_lo = nc.dram_tensor("aod_lo", [NOT, NKC, 128, 128], BF16).ap()
    xkf_hi = nc.dram_tensor("xkf_hi", [NKT, NKC, 128, 128], BF16).ap()
    xkf_lo = nc.dram_tensor("xkf_lo", [NKT, NKC, 128, 128], BF16).ap()
    xof_hi = nc.dram_tensor("xof_hi", [NOT, NKC, 128, 128], BF16).ap()
    xof_lo = nc.dram_tensor("xof_lo", [NOT, NKC, 128, 128], BF16).ap()
    h2nd = nc.dram_tensor("h2nd", [1024 + NSLT, E], F32).ap()
    idcw_d = nc.dram_tensor("idcw_d", [NEXP, CAP + 16, 2], F32).ap()
    y_d = nc.dram_tensor("y_d", [1024 + NSLT, E], F32).ap()

    handles = {}

    alloc_seq = []

    def sb(name, shape, dtype=F32):
        cm = nc.sbuf_tensor(name, shape, dtype)
        h = cm.__enter__()
        handles[name] = cm
        alloc_seq.append(name)
        return h

    def freeh(*names):
        for n in sorted(names, key=lambda n: -alloc_seq.index(n)):
            handles.pop(n).__exit__(None, None, None)

    ident = sb("ident", [128, 128], BF16)
    onesc = sb("onesc", [128, 1], BF16)
    onesf = sb("onesf", [128, 1])
    shiftt = sb("shiftt", [128, 1])
    onesr = sb("onesr", [1, 128])
    gwhi = sb("gwhi", [128, NKC, NEXP], BF16)
    gwlo = sb("gwlo", [128, NKC, NEXP], BF16)
    gwf = sb("gwf", [128, NKC, NEXP])
    combine = sb("combine", [128, NOT, NEXP])
    mu1 = sb("mu1", [128, NOT])
    s1v = sb("s1v", [128, NOT])
    gsvec = sb("gsvec", [128, 1])

    stg = sb("stg", [128, E])
    shi = sb("shi", [128, E], BF16)
    slo = sb("slo", [128, E], BF16)
    qnb = sb("qnb", [128, 128])
    knb = sb("knb", [128, 128])
    xfh = [sb("xfh0", [128, NKC, 256], BF16)] * 2
    xfl = [sb("xfl0", [128, NKC, 256], BF16)] * 2
    wt_hi = sb("wt_hi", [128, NKC, 512], BF16)
    wt_lo = sb("wt_lo", [128, NKC, 512], BF16)
    masks = sb("masks", [128, 16, 512], BF16)
    k_hi = sb("k_hi", [128, NKT, 128], BF16)
    k_lo = sb("k_lo", [128, NKT, 128], BF16)
    v_hi = sb("v_hi", [128, NKT, 128], BF16)
    v_lo = sb("v_lo", [128, NKT, 128], BF16)
    rks_hi = sb("rks_hi", [128, NKT, 128], BF16)
    rks_lo = sb("rks_lo", [128, NKT, 128], BF16)
    rqs_hi = sb("rqs_hi", [128, NKT, 128], BF16)
    rqs_lo = sb("rqs_lo", [128, NKT, 128], BF16)
    q_hi = sb("q_hi", [128, 4, NOT, 128], BF16)
    q_lo = sb("q_lo", [128, 4, NOT, 128], BF16)
    rqo_hi = sb("rqo_hi", [128, NOT, 128], BF16)
    rqo_lo = sb("rqo_lo", [128, NOT, 128], BF16)
    rko_hi = sb("rko_hi", [128, NOT, 128], BF16)
    rko_lo = sb("rko_lo", [128, NOT, 128], BF16)
    p_f = [sb(f"p_f{i}", [128, 512]) for i in range(2)]
    p_hi = [sb(f"p_hi{i}", [128, 512], BF16) for i in range(2)]
    p_lo = [sb(f"p_lo{i}", [128, 512], BF16) for i in range(2)]
    lrow = sb("lrow", [1, 512])
    lrec = sb("lrec", [1, 512])
    rbc = sb("rbc", [128, 512])
    aos = sb("aos", [128, 512])
    aosh = sb("aosh", [128, 512], BF16)
    aosl = sb("aosl", [128, 512], BF16)
    cost = [sb(f"cost{i}", [128, 128]) for i in range(2)]
    sint = [sb(f"sint{i}", [128, 128]) for i in range(2)]
    ropw = sb("ropw", [128, 3, 512])
    ssq = sb("ssq", [128, 8])
    rstd = sb("rstd", [128, 8])
    tmp1 = sb("tmp1", [128, 8])
    sqs = sb("sqs", [128, 128], BF16)
    sthi = sb("sthi", [128, 2, 128], BF16)
    stlo = sb("stlo", [128, 2, 128], BF16)

    pbank = [nc.alloc_psum_tensor(f"pb{i}", [128, 512], F32) for i in range(6)]
    tbank = [nc.alloc_psum_tensor(f"tb{i}", [128, 1024], BF16) for i in range(2)]

    T = {}

    pr.phase("consts")
    c0 = []
    c0.append(pr.op("sy", lambda e: e.dma_start(out=ident.ap()[:, :], in_=ideye)))
    for d8 in range(16):
        c0.append(pr.op("sy", lambda e, d8=d8: e.dma_start(
            out=masks.ap()[:, d8, :], in_=masks_in[d8, :, :])))
    c0.append(pr.op("sy", lambda e: e.dma_start(
        out=qnb.ap()[:, :], in_=qnw[None, :].broadcast_to([128, D]))))
    c0.append(pr.op("sy", lambda e: e.dma_start(
        out=knb.ap()[:, :], in_=knw[None, :].broadcast_to([128, D]))))
    tok_gwf = []
    for cq in range(4):
        tok_gwf.append(pr.op("sy", lambda e, cq=cq: e.dma_start(
            out=gwf.ap()[:, 4 * cq:4 * (cq + 1), :],
            in_=gate_w[512 * cq:512 * (cq + 1), :]
            .rearrange("(c p) x -> p c x", p=128))))
    c0.append(pr.op("ve", lambda e: e.memset(onesc.ap()[:, :], 1.0)))
    c0.append(pr.op("ve", lambda e: e.memset(onesf.ap()[:, :], 1.0)))
    c0.append(pr.op("ve", lambda e: e.memset(shiftt.ap()[:, :], EXP_SHIFT)))
    c0.append(pr.op("ve", lambda e: e.memset(onesr.ap()[:1, :], 1.0)))
    tgh = pr.op("se", lambda e: e.copy(gwhi.ap()[:, :, :], gwf.ap()[:, :, :]),
                waits=tok_gwf)
    tgl = pr.op("ve", lambda e: e.tensor_tensor(
        gwlo.ap()[:, :, :], gwf.ap()[:, :, :], gwhi.ap()[:, :, :], ALU.subtract),
        waits=tok_gwf + [tgh])
    c0 += [tgh, tgl]
    T["consts"] = c0

    def split3(src_ap, hi_ap, lo_ap, waits, eng_hi="ve"):
        th = pr.op(eng_hi, lambda e: e.tensor_copy(hi_ap, src_ap) if eng_hi == "ve"
                   else e.copy(hi_ap, src_ap), waits=waits)
        tl = pr.op("ve", lambda e: e.tensor_tensor(lo_ap, src_ap, hi_ap, ALU.subtract),
                   waits=_flat(waits) + [th])
        return th, tl

    def mm3(psum_ap, st_h, st_l, mv_h, mv_l, start, stop, waits, tpos=None,
            force_tok=False):
        kw = {} if tpos is None else {"tile_position": tpos}
        pr.op("te", lambda e: e.matmul(psum_ap, st_h, mv_h, start=start, stop=False, **kw),
              waits=waits, inc=False)
        pr.op("te", lambda e: e.matmul(psum_ap, st_h, mv_l, start=False, stop=False, **kw),
              inc=False)
        return pr.op("te", lambda e: e.matmul(psum_ap, st_l, mv_h, start=False, stop=stop, **kw),
                     inc=(stop or force_tok))

    def transpose16(src_hi, src_lo, evac_specs, waits):
        last = None
        for half in range(2):
            tts = []
            for ci in range(8):
                c = half * 8 + ci
                for hl, srct in ((0, src_hi), (1, src_lo)):
                    tts.append(pr.op("te", lambda e, c=c, ci=ci, hl=hl, srct=srct:
                               e.transpose(tbank[hl].ap()[:, ci * 128:(ci + 1) * 128],
                                           srct.ap()[:, c * 128:(c + 1) * 128],
                                           ident.ap()[:, :]),
                               waits=_flat(waits) + [T.get(("tbank", hl))]))
            for hl in range(2):
                dst = evac_specs(half, hl)
                tev = pr.op("ve", lambda e, hl=hl, dst=dst:
                            e.tensor_copy(
                                dst,
                                tbank[hl].ap()[:, :].rearrange("p (c k) -> p c k",
                                                               k=128)),
                            waits=tts)
                last = tev
                T[("tbank", hl)] = tev
        return last

    def load_xf(co, xsh, xsl, tt, war):
        toks = []
        for qc in range(4):
            for (xs, dstt) in ((xsh, xfh[0]), (xsl, xfl[0])):
                toks.append(pr.op("sy", lambda e, qc=qc, xs=xs, dstt=dstt, tt=tt,
                            co=co:
                            e.dma_start(
                                out=dstt.ap()[:, 4 * qc:4 * (qc + 1), co:co + 128],
                                in_=xs[tt, 4 * qc:4 * (qc + 1)]
                                .rearrange("c d k -> d c k")),
                            waits=_flat([war])))
        return toks

    def build_wt(Wsrc, r0, wcol, extra=()):
        stgv = stg.ap().rearrange("p (c o) -> p c o", o=128)
        toks = []
        for cq in range(4):
            toks.append(pr.op("sy", lambda e, cq=cq, Wsrc=Wsrc, r0=r0:
                        e.dma_start(
                out=stgv[:, 4 * cq:4 * (cq + 1), :],
                in_=Wsrc[512 * cq:512 * (cq + 1), r0:r0 + 128]
                .rearrange("(c p) o -> p c o", p=128)),
                waits=[T.get(("stg",))]))
        th, tl = split3(stgv[:, :, :],
                        wt_hi.ap()[:, :, wcol:wcol + 128],
                        wt_lo.ap()[:, :, wcol:wcol + 128],
                        toks + [T.get(("wts",))] + _flat(extra), eng_hi="se")
        T[("stg",)] = tl
        T[("wts",)] = tl
        return tl

    def rope_tm(src_ap, dst_ap, ct, st_, rst_scalar, nb_ap, waits, out_ta=None,
                bp=0):
        kn = ropw.ap()[:, bp, 0:128]
        tt1 = ropw.ap()[:, bp, 128:256]
        ta = pr.op("ve", lambda e: e.scalar_tensor_tensor(
            kn, src_ap, rst_scalar, nb_ap, ALU.mult, ALU.mult), waits=waits)
        if out_ta is not None:
            out_ta.append(ta)
        tb_ = pr.op("ve", lambda e: e.tensor_tensor(tt1, kn, ct, ALU.mult), waits=[ta])
        tu1 = pr.op("ve", lambda e: e.tensor_tensor(
            ropw.ap()[:, bp, 256:320], kn[:, 64:128], st_[:, 0:64], ALU.mult), waits=[ta])
        tu2 = pr.op("ve", lambda e: e.tensor_tensor(
            ropw.ap()[:, bp, 320:384], kn[:, 0:64], st_[:, 64:128], ALU.mult), waits=[ta])
        to1 = pr.op("ve", lambda e: e.tensor_tensor(
            dst_ap[:, 0:64], tt1[:, 0:64], ropw.ap()[:, bp, 256:320], ALU.subtract),
            waits=[tb_, tu1])
        to2 = pr.op("ve", lambda e: e.tensor_tensor(
            dst_ap[:, 64:128], tt1[:, 64:128], ropw.ap()[:, bp, 320:384], ALU.add),
            waits=[tb_, tu2])
        return [to1, to2]


    def drain_dma():
        toks = [(nm, pr.cnt[nm]) for q in ("sy", "gp") for nm in pr.pool[q]
                if pr.cnt[nm]]
        pr.op("sy", lambda e: e.nop(), waits=toks, inc=False)

    pr.phase("xscratch")
    for (xdram, xdsth, xdstl, ntt) in ((x_keys, xkf_hi, xkf_lo, NKT),
                                       (x_own, xof_hi, xof_lo, NOT)):
        for tt in range(ntt):
            co = 128 * (tt % 2)
            tok_d = pr.op("sy", lambda e, tt=tt, xdram=xdram: e.dma_start(
                out=stg.ap()[:, :], in_=xdram[tt * 128:(tt + 1) * 128, :]),
                waits=[T.get(("stg",))])
            th, tl = split3(stg.ap()[:, :], shi.ap()[:, :], slo.ap()[:, :],
                            [tok_d, T.get(("shilo",))], eng_hi="se")
            T[("stg",)] = tl

            def spec(half, hl, co=co):
                dtile = xfh[0] if hl == 0 else xfl[0]
                return dtile.ap()[:, half * 8:(half + 1) * 8, co:co + 128]
            tev = transpose16(shi, slo, spec, [th, tl, T.get(("xsc", co))])
            T[("shilo",)] = tev
            w1 = pr.op("gp", lambda e, tt=tt, co=co, xdsth=xdsth: e.dma_start(
                out=xdsth[tt].rearrange("c d k -> d c k"),
                in_=xfh[0].ap()[:, :, co:co + 128]), waits=[tev])
            w2 = pr.op("gp", lambda e, tt=tt, co=co, xdstl=xdstl: e.dma_start(
                out=xdstl[tt].rearrange("c d k -> d c k"),
                in_=xfl[0].ap()[:, :, co:co + 128]), waits=[tev])
            T[("xsc", co)] = [w1, w2]
    tok_xsc = [(nm, pr.cnt[nm]) for nm in pr.pool["gp"] if pr.cnt[nm]]

    for g in range(4):
        pr.phase(f"prep{g}")
        tok_gs = None
        if g < 3:
            gst = []
            for hq in range(4):
                gst.append(pr.op("sy", lambda e, g=g, hq=hq: e.dma_start(
                    out=gsvec.ap()[32 * hq:32 * (hq + 1), 0:1],
                    in_=gsc[4 * g + hq:4 * g + hq + 1][None, :]
                    .broadcast_to([32, 1])),
                    waits=[T.get(("gsv",))]))
            tok_gs = gst
            if g == 0:
                tok_gs = [pr.op("se", lambda e: e.mul(
                    gsvec.ap()[:, :], gsvec.ap()[:, :], 0.5), waits=gst)]
        war_te = [pr.tok("te")]
        t1 = build_wt(Wk, g * 128, 0, extra=war_te)
        t2 = build_wt(Wv, g * 128, 128, extra=war_te)
        t3 = t4 = None
        if g < 3:
            t3 = build_wt(Wrk, g * 128, 256, extra=war_te)
            t4 = build_wt(Wrq, g * 128, 384, extra=war_te)
        tok_wt = [t1, t2, t3, t4]

        for tt in range(NKT):
            co = 128 * (tt % 2)
            hp = tt % 2
            stoks = load_xf(co, xkf_hi, xkf_lo, tt,
                            [T.get(("xf", co))] + tok_xsc)
            ps = pbank[4 + hp]
            tokm = None
            for c in range(NKC):
                tokm = mm3(ps.ap()[:, 0:256],
                           xfh[0].ap()[:, c, co:co + 128],
                           xfl[0].ap()[:, c, co:co + 128],
                           wt_hi.ap()[:, c, 0:256], wt_lo.ap()[:, c, 0:256],
                           start=(c == 0), stop=(c == NKC - 1),
                           waits=stoks + [tok_wt, T.get(("pswar", 4 + hp))])
            tssq = pr.op("se", lambda e, ps=ps, hp=hp: e.activation(
                sqs.ap()[:, 0:128], ps.ap()[:, 0:128], AF.Square,
                accum_out=ssq.ap()[:, hp:hp + 1]), waits=[tokm, T.get(("ssq", hp))])
            tv = pr.op("ve", lambda e, hp=hp: e.tensor_scalar(
                tmp1.ap()[:, hp:hp + 1], ssq.ap()[:, hp:hp + 1], 1.0 / D, RMS_EPS,
                ALU.mult, ALU.add), waits=[tssq])
            tv = pr.op("ve", lambda e, hp=hp: e.reciprocal(
                rstd.ap()[:, hp:hp + 1], tmp1.ap()[:, hp:hp + 1]), waits=[tv])
            tv = pr.op("se", lambda e, hp=hp: e.activation(
                rstd.ap()[:, 4 + hp:5 + hp], rstd.ap()[:, hp:hp + 1], AF.Sqrt),
                waits=[tv])
            T[("ssq", hp)] = tv
            tok_ck = pr.op("sy", lambda e, tt=tt: e.dma_start(
                out=cost[tt % 2].ap()[:, :], in_=cosk_in[tt, :, :]),
                waits=[T.get(("cosk", tt % 2))])
            tok_sk = pr.op("sy", lambda e, tt=tt: e.dma_start(
                out=sint[tt % 2].ap()[:, :], in_=sink_in[tt, :, :]),
                waits=[T.get(("cosk", tt % 2))])
            rt = rope_tm(ps.ap()[:, 0:128], ropw.ap()[:, hp, 384:512],
                         cost[tt % 2].ap()[:, :], sint[tt % 2].ap()[:, :],
                         rstd.ap()[:, 4 + hp:5 + hp], knb.ap()[:, :],
                         [tv, tok_ck, tok_sk, T.get(("ropw", hp))], bp=hp)
            th, tl = split3(ropw.ap()[:, hp, 384:512], sthi.ap()[:, hp, :],
                            stlo.ap()[:, hp, :],
                            rt + [T.get(("stkv", hp))], eng_hi="se")
            ttr = []
            for hl, srct in ((0, sthi), (1, stlo)):
                ttr.append(pr.op("te", lambda e, hl=hl, srct=srct, hp=hp:
                           e.transpose(tbank[0].ap()[:, (2 * hp + hl) * 128:(2 * hp + hl + 1) * 128],
                                       srct.ap()[:, hp, :], ident.ap()[:, :]),
                           waits=[th, tl, T.get(("tbank", 0))]))
            tev_h = pr.op("ve", lambda e, tt=tt, hp=hp: e.tensor_copy(
                k_hi.ap()[:, tt, :], tbank[0].ap()[:, 2 * hp * 128:(2 * hp + 1) * 128]), waits=ttr)
            tev_l = pr.op("ve", lambda e, tt=tt, hp=hp: e.tensor_copy(
                k_lo.ap()[:, tt, :], tbank[0].ap()[:, (2 * hp + 1) * 128:(2 * hp + 2) * 128]), waits=ttr)
            T[("tbank", 0)] = tev_l
            T[("stkv", hp)] = tev_l
            T[("ropw", hp)] = tev_l
            T[("cosk", tt % 2)] = rt[-1]
            tvh = pr.op("se", lambda e, tt=tt, ps=ps: e.copy(
                v_hi.ap()[:, tt, :], ps.ap()[:, 128:256]), waits=[tokm, tssq])
            tvl = pr.op("ve", lambda e, tt=tt, ps=ps: e.tensor_tensor(
                v_lo.ap()[:, tt, :], ps.ap()[:, 128:256], v_hi.ap()[:, tt, :],
                ALU.subtract), waits=[tokm, tvh])
            T[("pswar", 4 + hp)] = [tvl, tssq, rt[-1]]
            if g < 3:
                specs = [(256, rks_hi, rks_lo, pbank[0], 0)]
                if g == 0:
                    specs.append((384, rqs_hi, rqs_lo, pbank[1], 1))
                for (wcol, dsth, dstl, ps2, bk) in specs:
                    tokm2 = None
                    for c in range(NKC):
                        tokm2 = mm3(ps2.ap()[:, 0:128],
                                    wt_hi.ap()[:, c, wcol:wcol + 128],
                                    wt_lo.ap()[:, c, wcol:wcol + 128],
                                    xfh[0].ap()[:, c, co:co + 128],
                                    xfl[0].ap()[:, c, co:co + 128],
                                    start=(c == 0), stop=(c == NKC - 1),
                                    waits=stoks + [tok_wt, T.get(("pswar", bk))])
                    tsc = pr.op("ve", lambda e, ps2=ps2, bk=bk: e.tensor_scalar(
                        ropw.ap()[:, 2, 128 * bk:128 * bk + 128],
                        ps2.ap()[:, 0:128], gsvec.ap()[:, 0:1],
                        None, ALU.mult),
                        waits=[tokm2, T.get(("ropw2", bk))] + _flat([tok_gs]))
                    th2, tl2 = split3(ropw.ap()[:, 2, 128 * bk:128 * bk + 128],
                                      dsth.ap()[:, tt, :], dstl.ap()[:, tt, :],
                                      [tsc], eng_hi="se")
                    T[("pswar", bk)] = tl2
                    T[("ropw2", bk)] = tl2
            T[("xf", co)] = pr.tok("te")

        if g < 3:
            for tt in range(NOT):
                co = 128 * (tt % 2)
                stoks = load_xf(co, xof_hi, xof_lo, tt,
                                [T.get(("xf", co))] + tok_xsc)
                specs = [(384, rqo_hi, rqo_lo, pbank[0], 0)]
                if g == 0:
                    specs.append((256, rko_hi, rko_lo, pbank[1], 1))
                for (wcol, dsth, dstl, ps2, bk) in specs:
                    tokm2 = None
                    for c in range(NKC):
                        tokm2 = mm3(ps2.ap()[:, 0:128],
                                    wt_hi.ap()[:, c, wcol:wcol + 128],
                                    wt_lo.ap()[:, c, wcol:wcol + 128],
                                    xfh[0].ap()[:, c, co:co + 128],
                                    xfl[0].ap()[:, c, co:co + 128],
                                    start=(c == 0), stop=(c == NKC - 1),
                                    waits=stoks + [T.get(("pswar", bk))])
                    th2, tl2 = split3(ps2.ap()[:, 0:128],
                                      dsth.ap()[:, tt, :], dstl.ap()[:, tt, :],
                                      [tokm2], eng_hi="se")
                    T[("pswar", bk)] = tl2
                T[("xf", co)] = pr.tok("te")

        war_te = [pr.tok("te")]
        tok_wq = [build_wt(Wq, 512 * g + 128 * oi, 128 * oi, extra=war_te)
                  for oi in range(4)]
        for tt in range(NOT):
            co = 128 * (tt % 2)
            stoks = load_xf(co, xof_hi, xof_lo, tt,
                            [T.get(("xf", co))] + tok_xsc)
            ps = pbank[4 + (tt % 2)]
            tokm = None
            for c in range(NKC):
                tokm = mm3(ps.ap()[:, 0:512],
                           xfh[0].ap()[:, c, co:co + 128], xfl[0].ap()[:, c, co:co + 128],
                           wt_hi.ap()[:, c, 0:512], wt_lo.ap()[:, c, 0:512],
                           start=(c == 0), stop=(c == NKC - 1),
                           waits=stoks + [tok_wq, T.get(("pswar", 4 + tt % 2))])
            tok_cq = pr.op("sy", lambda e, tt=tt: e.dma_start(
                out=cost[tt % 2].ap()[:, :], in_=cosq_in[tt, :, :]),
                waits=[T.get(("cosk", tt % 2))])
            tok_sq = pr.op("sy", lambda e, tt=tt: e.dma_start(
                out=sint[tt % 2].ap()[:, :], in_=sinq_in[tt, :, :]),
                waits=[T.get(("cosk", tt % 2))])
            last = None
            prev_ta = []
            for h in range(4):
                hp = h % 2
                tssq = pr.op("se", lambda e, ps=ps, h=h, hp=hp: e.activation(
                    sqs.ap()[:, 0:128],
                    ps.ap()[:, h * 128:(h + 1) * 128],
                    AF.Square, accum_out=ssq.ap()[:, hp:hp + 1]),
                    waits=[tokm, T.get(("ssq", hp))] + prev_ta)
                prev_ta = []
                tv = pr.op("ve", lambda e, hp=hp: e.tensor_scalar(
                    tmp1.ap()[:, hp:hp + 1], ssq.ap()[:, hp:hp + 1], 1.0 / D, RMS_EPS,
                    ALU.mult, ALU.add), waits=[tssq])
                tv = pr.op("ve", lambda e, hp=hp: e.reciprocal(
                    tmp1.ap()[:, hp:hp + 1], tmp1.ap()[:, hp:hp + 1]), waits=[tv])
                tv = pr.op("se", lambda e, hp=hp: e.activation(
                    rstd.ap()[:, 4 + hp:5 + hp], tmp1.ap()[:, hp:hp + 1], AF.Sqrt,
                    scale=float(1.0 / D)), waits=[tv])
                T[("ssq", hp)] = tv
                rt = rope_tm(ps.ap()[:, h * 128:(h + 1) * 128], ropw.ap()[:, hp, 384:512],
                             cost[tt % 2].ap()[:, :], sint[tt % 2].ap()[:, :],
                             rstd.ap()[:, 4 + hp:5 + hp], qnb.ap()[:, :],
                             [tv, tok_cq, tok_sq, T.get(("ropw", hp))],
                             out_ta=prev_ta, bp=hp)
                th, tl = split3(ropw.ap()[:, hp, 384:512], sthi.ap()[:, hp, :],
                                stlo.ap()[:, hp, :],
                                rt + [T.get(("stkv", hp))], eng_hi="se")
                ttr = []
                for hl, srct in ((0, sthi), (1, stlo)):
                    ttr.append(pr.op("te", lambda e, hl=hl, srct=srct, hp=hp:
                               e.transpose(tbank[0].ap()[:, (2 * hp + hl) * 128:(2 * hp + hl + 1) * 128],
                                           srct.ap()[:, hp, :], ident.ap()[:, :]),
                               waits=[th, tl, T.get(("tbank", 0))]))
                teh = pr.op("ve", lambda e, h=h, tt=tt, hp=hp: e.tensor_copy(
                    q_hi.ap()[:, h, tt, :], tbank[0].ap()[:, 2 * hp * 128:(2 * hp + 1) * 128]), waits=ttr)
                tel = pr.op("ve", lambda e, h=h, tt=tt, hp=hp: e.tensor_copy(
                    q_lo.ap()[:, h, tt, :], tbank[0].ap()[:, (2 * hp + 1) * 128:(2 * hp + 2) * 128]), waits=ttr)
                T[("tbank", 0)] = tel
                T[("stkv", hp)] = tel
                T[("ropw", hp)] = tel
                last = tel
            T[("cosk", tt % 2)] = last
            T[("pswar", 4 + tt % 2)] = [last, pr.tok("se")]
            T[("xf", co)] = pr.tok("te")

        pr.phase(f"attn{g}")
        sT = [pbank[0], pbank[1]]
        pout = pbank[2]
        pl = pbank[3]
        for q2 in range(2):
            for h in range(4):
                ch = 4 * g + h
                nch = 16 * q2 + 16
                tok_out_last = None
                tok_l_last = None

                def emit_sT(c, h=h, q2=q2):
                    sb_i = c % 2
                    ps = sT[sb_i]
                    qmv_h = (q_hi.ap()[:, h, 4 * q2:4 * q2 + 4, :]
                             .rearrange("p a b -> p (a b)"))
                    qmv_l = (q_lo.ap()[:, h, 4 * q2:4 * q2 + 4, :]
                             .rearrange("p a b -> p (a b)"))
                    tokqk = mm3(ps.ap()[:, 0:512],
                                k_hi.ap()[:, c, :], k_lo.ap()[:, c, :],
                                qmv_h, qmv_l, start=True, stop=(g == 3),
                                waits=[T.get(("sT", sb_i))])
                    if g < 3:
                        bp = 32 * h
                        tp = (bp, 0)
                        tokqk = mm3(ps.ap()[:, 0:512],
                                    rks_hi.ap()[bp:bp + 32, c, :],
                                    rks_lo.ap()[bp:bp + 32, c, :],
                                    rqo_hi.ap()[bp:bp + 32, 4 * q2:4 * q2 + 4, :]
                                    .rearrange("p a b -> p (a b)"),
                                    rqo_lo.ap()[bp:bp + 32, 4 * q2:4 * q2 + 4, :]
                                    .rearrange("p a b -> p (a b)"),
                                    start=False, stop=(g != 0), waits=[], tpos=tp)
                        if g == 0:
                            tokqk = mm3(ps.ap()[:, 0:512],
                                        rqs_hi.ap()[bp:bp + 32, c, :],
                                        rqs_lo.ap()[bp:bp + 32, c, :],
                                        rko_hi.ap()[bp:bp + 32, 4 * q2:4 * q2 + 4, :]
                                        .rearrange("p a b -> p (a b)"),
                                        rko_lo.ap()[bp:bp + 32, 4 * q2:4 * q2 + 4, :]
                                        .rearrange("p a b -> p (a b)"),
                                        start=False, stop=True, waits=[], tpos=tp)
                    return tokqk

                tok_sT = {0: emit_sT(0)}
                for c in range(nch):
                    sb_i = c % 2
                    ps = sT[sb_i]
                    pbuf = c % 2
                    tokexp = pr.op("se", lambda e, ps=ps, pbuf=pbuf: e.activation(
                        p_f[pbuf].ap()[:, :], ps.ap()[:, 0:512], AF.Exp,
                        bias=shiftt.ap()[:, :]),
                        waits=[tok_sT[c], T.get(("p_f", pbuf))])
                    T[("sT", sb_i)] = tokexp
                    if c + 1 < nch:
                        tok_sT[c + 1] = emit_sT(c + 1)
                    dlt = c - 16 * q2
                    tokp = tokexp
                    if dlt >= 0:
                        tokp = pr.op("ve", lambda e, pbuf=pbuf, dlt=dlt: e.tensor_tensor(
                            p_f[pbuf].ap()[:, :], p_f[pbuf].ap()[:, :],
                            masks.ap()[:, dlt, :], ALU.mult), waits=[tokexp])
                    th, tl = split3(p_f[pbuf].ap()[:, :], p_hi[pbuf].ap()[:, :],
                                    p_lo[pbuf].ap()[:, :],
                                    [tokp, T.get(("p_s", pbuf))], eng_hi="se")
                    T[("p_f", pbuf)] = tl
                    tok_out_last = mm3(pout.ap()[:, 0:512],
                                       v_hi.ap()[:, c, :], v_lo.ap()[:, c, :],
                                       p_hi[pbuf].ap()[:, :], p_lo[pbuf].ap()[:, :],
                                       start=(c == 0), stop=(c == nch - 1),
                                       waits=[th, tl, T.get(("pout",))])
                    tok_l_last = pr.op("te", lambda e, pbuf=pbuf, c=c, nch=nch:
                                       e.matmul(
                        pl.ap()[:1, 0:512], onesf.ap()[:, :], p_f[pbuf].ap()[:, :],
                        start=(c == 0), stop=(c == nch - 1)),
                        waits=[tokp] + ([T.get(("pl",))] if c == 0 else []),
                        inc=(c == nch - 1))
                    T[("p_f", pbuf)] = [T[("p_f", pbuf)], tok_l_last]
                    T[("p_s", pbuf)] = pr.tok("te")
                tcp = pr.op("ve", lambda e: e.tensor_copy(
                    lrow.ap()[:1, :], pl.ap()[:1, 0:512]), waits=[tok_l_last])
                trc = pr.op("ve", lambda e: e.reciprocal(
                    lrec.ap()[:1, :], lrow.ap()[:1, :]), waits=[tcp])
                tbc = pr.op("te", lambda e: e.matmul(
                    pl.ap()[:, 0:512], onesr.ap()[:1, :], lrec.ap()[:1, :],
                    start=True, stop=True), waits=[trc, tcp])
                tev = pr.op("ve", lambda e: e.tensor_copy(
                    rbc.ap()[:, :], pl.ap()[:, 0:512]), waits=[tbc])
                T[("pl",)] = tev
                tsc = pr.op("ve", lambda e: e.tensor_tensor(
                    aos.ap()[:, :], pout.ap()[:, 0:512], rbc.ap()[:, :], ALU.mult),
                    waits=[tev, tok_out_last, T.get(("aosr",))])
                T[("pout",)] = tsc
                th, tl = split3(aos.ap()[:, :], aosh.ap()[:, :], aosl.ap()[:, :],
                                [tsc, T.get(("aodma",))])
                tda = []
                for sti in range(4):
                    tda.append(pr.op("sy", lambda e, ch=ch, q2=q2, sti=sti:
                               e.dma_start(
                        out=aod_hi[4 * q2 + sti, ch],
                        in_=aosh.ap()[:, 128 * sti:128 * (sti + 1)]),
                        waits=[th, tl]))
                    tda.append(pr.op("sy", lambda e, ch=ch, q2=q2, sti=sti:
                               e.dma_start(
                        out=aod_lo[4 * q2 + sti, ch],
                        in_=aosl.ap()[:, 128 * sti:128 * (sti + 1)]),
                        waits=[tl]))
                T[("aodma",)] = tda
                T[("aosr",)] = tl

    tok_ao_done = [(nm, pr.cnt[nm]) for nm in pr.pool["sy"] if pr.cnt[nm]]

    pr.phase("drain_a")
    drain_dma()
    pr.phase("wo")
    freeh("qnb", "knb", "xfh0", "xfl0",
          "wt_hi", "wt_lo", "masks", "k_hi", "k_lo", "v_hi", "v_lo",
          "rks_hi", "rks_lo", "rqs_hi", "rqs_lo", "q_hi", "q_lo",
          "rqo_hi", "rqo_lo", "rko_hi", "rko_lo",
          "p_f0", "p_f1", "p_hi0", "p_hi1", "p_lo0", "p_lo1",
          "lrow", "lrec", "rbc", "aos", "aosh", "aosl",
          "cost0", "cost1", "sint0", "sint1",
          "ropw", "ssq", "rstd", "tmp1", "sqs", "sthi", "stlo")
    dTMi = sb("dTMi", [128, NOT, NEXP], I32)
    payl = sb("payl", [128, NOT, NEXP, 2])
    lnss = sb("lnss", [128, 1])
    lnsm = sb("lnsm", [128, 1])
    lnt = sb("lnt", [128, 1])
    r1p = [sb(f"r1p{i}", [128, E]) for i in range(2)]
    aosth = [sb("aosth0", [128, NKC, 128], BF16)] * 2
    aostl = [sb("aostl0", [128, NKC, 128], BF16)] * 2
    xot = [sb(f"xot{i}", [128, 512]) for i in range(2)]
    hstage = sb("hstage", [128, E])
    hfst_hi = sb("hfst_hi", [128, NKC, 128], BF16)
    hfst_lo = sb("hfst_lo", [128, NKC, 128], BF16)
    cmbT = sb("cmbT", [8, 1024])
    flgt = sb("flgt", [8, 1024])
    post = sb("post", [8, 1024])
    d0t = post
    zzt = flgt
    dTM = sb("dTM", [128, NOT, NEXP])
    tokid = sb("tokid_s", [128, NOT])
    erowb = sb("erowb", [128, NEXP])
    initt = sb("initt", [128, 50])
    prb = sb("prb", [128, NEXP])
    prb2 = sb("prb2", [128, NEXP])
    prb3 = sb("prb3", [128, NEXP])
    cmx = sb("cmx", [128, 1])
    cm2 = sb("cm2", [128, 1])
    csum = sb("csum", [128, 1])
    cmx8 = sb("cmx8", [128, NEXP])
    identf = sb("identf", [128, 128])
    woth = sb("woth", [128, NKC, 2048], BF16)
    wotl = sb("wotl", [128, NKC, 2048], BF16)

    tin = pr.op("sy", lambda e: e.dma_start(
        out=initt.ap()[:, :],
        in_=moe_init.rearrange("e c two -> (e c two)").rearrange("(p a) -> p a", p=128)))
    tin2 = pr.op("sy", lambda e: e.dma_start(out=idcw_d, in_=initt.ap()[:, :]),
                 waits=[tin])
    tidf = pr.op("ve", lambda e: e.tensor_copy(identf.ap()[:, :], ident.ap()[:, :]))
    ttk = pr.op("sy", lambda e: e.dma_start(
        out=tokid.ap()[:, :], in_=tokid_in.rearrange("(a p) -> p a", p=128),
        allow_slow_non_contiguous=True))
    ter = pr.op("sy", lambda e: e.dma_start(
        out=erowb.ap()[:, :], in_=erow_in[None, :].broadcast_to([128, NEXP])))
    tzz = None
    T[("cmx8init",)] = pr.op("ve", lambda e: e.memset(cmx8.ap()[:, :], -1.0))

    stgv = stg.ap().rearrange("p (c o) -> p c o", o=128)
    for blk in range(16):
        toks = []
        for cq in range(4):
            toks.append(pr.op("sy", lambda e, cq=cq, blk=blk: e.dma_start(
                out=stgv[:, 4 * cq:4 * (cq + 1), :],
                in_=Wo[512 * cq:512 * (cq + 1), 128 * blk:128 * (blk + 1)]
                .rearrange("(c p) o -> p c o", p=128)),
                waits=[T.get(("stg",))]))
        th, tl = split3(stgv[:, :, :],
                        woth.ap()[:, :, 128 * blk:128 * (blk + 1)],
                        wotl.ap()[:, :, 128 * blk:128 * (blk + 1)],
                        toks + [T.get(("wts",))], eng_hi="se")
        T[("stg",)] = tl
        T[("wts",)] = tl
    tok_wot = tl

    def post_st(st):
        bi = st % 2
        w8 = T[("r1d", st)]
        tq = pr.op("se", lambda e, bi=bi: e.activation(
            hstage.ap()[:, :], r1p[bi].ap()[:, :], AF.Square,
            accum_out=lnss.ap()[:, 0:1]),
            waits=w8 + [T.get(("lnx",)), T.get(("hstg",))])
        tsm = pr.op("ve", lambda e, bi=bi: e.reduce_sum(
            lnsm.ap()[:, 0:1], r1p[bi].ap()[:, :], axis=AX.X),
            waits=w8 + [T.get(("lnx",))])
        tmu = pr.op("ve", lambda e, st=st: e.tensor_scalar(
            mu1.ap()[:, st:st + 1], lnsm.ap()[:, 0:1], 1.0 / E, None, ALU.mult),
            waits=[tsm])
        tms = pr.op("ve", lambda e: e.tensor_scalar(
            lnss.ap()[:, 0:1], lnss.ap()[:, 0:1], 1.0 / E, LN_EPS, ALU.mult, ALU.add),
            waits=[tq])
        tm2 = pr.op("ve", lambda e, st=st: e.tensor_tensor(
            lnt.ap()[:, 0:1], mu1.ap()[:, st:st + 1], mu1.ap()[:, st:st + 1],
            ALU.mult), waits=[tmu])
        tvr = pr.op("ve", lambda e: e.tensor_tensor(
            lnt.ap()[:, 0:1], lnss.ap()[:, 0:1], lnt.ap()[:, 0:1], ALU.subtract),
            waits=[tms, tm2])
        trc = pr.op("ve", lambda e: e.reciprocal(lnt.ap()[:, 0:1], lnt.ap()[:, 0:1]),
                    waits=[tvr])
        tsq = pr.op("se", lambda e, st=st: e.activation(
            s1v.ap()[:, st:st + 1], lnt.ap()[:, 0:1], AF.Sqrt), waits=[trc])
        T[("lnx",)] = tsq
        th2 = pr.op("ve", lambda e, st=st, bi=bi: e.tensor_scalar(
            hstage.ap()[:, :], r1p[bi].ap()[:, :], mu1.ap()[:, st:st + 1],
            s1v.ap()[:, st:st + 1], ALU.subtract, ALU.mult),
            waits=w8 + [tsq, T.get(("hstg",))])
        T[("r1w", bi)] = [tq, tsm, th2]
        td = pr.op("sy", lambda e, st=st: e.dma_start(
            out=h2nd[st * 128:(st + 1) * 128, :], in_=hstage.ap()[:, :]),
            waits=[th2])
        sh2, sl2 = split3(hstage.ap()[:, :], shi.ap()[:, :], slo.ap()[:, :],
                          [th2, T.get(("shilo",))])
        T[("hstg",)] = [td, sl2]

        def spec_h(half, hl):
            dt_ = hfst_hi if hl == 0 else hfst_lo
            return dt_.ap()[:, half * 8:(half + 1) * 8, :]
        tevh = transpose16(shi, slo, spec_h, [sh2, sl2, T.get(("hfstw",))])
        T[("shilo",)] = tevh
        tokm = None
        for c in range(NKC):
            tokm = mm3(pbank[5].ap()[:, 0:NEXP],
                       hfst_hi.ap()[:, c, :], hfst_lo.ap()[:, c, :],
                       gwhi.ap()[:, c, :], gwlo.ap()[:, c, :],
                       start=(c == 0), stop=(c == NKC - 1),
                       waits=[tevh] + T["consts"] + [T.get(("gps",))])
        T[("hfstw",)] = tokm
        tmx = pr.op("ve", lambda e: e.reduce_max(
            cmx.ap()[:, :], pbank[5].ap()[:, 0:NEXP], axis=AX.X), waits=[tokm])
        tng = pr.op("ve", lambda e: e.tensor_scalar(
            cm2.ap()[:, :], cmx.ap()[:, :], -1.0, None, ALU.mult), waits=[tmx])
        tex = pr.op("se", lambda e: e.activation(
            prb.ap()[:, :], pbank[5].ap()[:, 0:NEXP], AF.Exp, bias=cm2.ap()[:, :]),
            waits=[tng, T.get(("prb",))])
        T[("gps",)] = tex
        tsm2 = pr.op("ve", lambda e: e.reduce_sum(
            csum.ap()[:, :], prb.ap()[:, :], axis=AX.X), waits=[tex])
        trc2 = pr.op("ve", lambda e: e.reciprocal(
            csum.ap()[:, :], csum.ap()[:, :]), waits=[tsm2])
        tpr = pr.op("ve", lambda e: e.tensor_scalar(
            prb.ap()[:, :], prb.ap()[:, :], csum.ap()[:, :], None, ALU.mult),
            waits=[trc2, tex])
        tm8 = pr.op("ve", lambda e: e.max(
            prb2.ap()[:, :], prb.ap()[:, :]), waits=[tpr, T.get(("prb2",))])
        tcp8 = pr.op("ve", lambda e: e.tensor_copy(
            cmx8.ap()[:, 0:2], prb2.ap()[:, 0:2]),
            waits=[tm8, T[("cmx8init",)], T.get(("cmx8",))])
        tmr = pr.op("ve", lambda e: e.match_replace(
            prb3.ap()[:, :], cmx8.ap()[:, :], prb.ap()[:, :], 0.0),
            waits=[tcp8, tpr])
        T[("cmx8",)] = tmr
        tden = pr.op("ve", lambda e: e.tensor_tensor(
            csum.ap()[:, :], prb2.ap()[:, 0:1], prb2.ap()[:, 1:2], ALU.add),
            waits=[tm8, trc2])
        trd = pr.op("ve", lambda e: e.reciprocal(
            csum.ap()[:, :], csum.ap()[:, :]), waits=[tden])
        tdf = pr.op("ve", lambda e: e.tensor_tensor(
            prb2.ap()[:, :], prb.ap()[:, :], prb3.ap()[:, :], ALU.subtract),
            waits=[tmr, tpr])
        T[("prb2",)] = tdf
        tcb = pr.op("ve", lambda e, st=st: e.tensor_scalar(
            combine.ap()[:, st, :], prb2.ap()[:, :], csum.ap()[:, :], None,
            ALU.mult), waits=[trd, tdf])
        T[("prb",)] = tcb
        T[("cmb", st)] = tcb

    for st in range(NOT):
        bi = st % 2
        tld = []
        for cq in range(4):
            tld.append(pr.op("sy", lambda e, st=st, cq=cq, bi=bi: e.dma_start(
                out=aosth[bi].ap()[:, 4 * cq:4 * (cq + 1), :],
                in_=aod_hi[st, 4 * cq:4 * (cq + 1)]
                .rearrange("c d k -> d c k")),
                waits=[tok_ao_done, T.get(("aost", 0))]))
            tld.append(pr.op("sy", lambda e, st=st, cq=cq, bi=bi: e.dma_start(
                out=aostl[bi].ap()[:, 4 * cq:4 * (cq + 1), :],
                in_=aod_lo[st, 4 * cq:4 * (cq + 1)]
                .rearrange("c d k -> d c k")),
                waits=[tok_ao_done, T.get(("aost", 0))]))
        tr1s = []
        tokm = None
        for ob in range(4):
            ps = pbank[ob % 2]
            tok_x = pr.op("sy", lambda e, st=st, ob=ob: e.dma_start(
                out=xot[ob % 2].ap()[:, :],
                in_=x_own[st * 128:(st + 1) * 128, 512 * ob:512 * (ob + 1)]),
                waits=[T.get(("xot", ob % 2))])
            for c in range(NKC):
                tokm = mm3(ps.ap()[:, 0:512],
                           aosth[bi].ap()[:, c, :], aostl[bi].ap()[:, c, :],
                           woth.ap()[:, c, 512 * ob:512 * (ob + 1)],
                           wotl.ap()[:, c, 512 * ob:512 * (ob + 1)],
                           start=(c == 0), stop=(c == NKC - 1),
                           waits=tld + [tok_wot, T.get(("wops", ob % 2))])
            tr1 = pr.op("ve", lambda e, ps=ps, ob=ob, bi=bi: e.tensor_tensor(
                r1p[bi].ap()[:, ob * 512:(ob + 1) * 512], ps.ap()[:, 0:512],
                xot[ob % 2].ap()[:, :], ALU.add),
                waits=[tokm, tok_x, T.get(("r1w", bi))])
            T[("wops", ob % 2)] = tr1
            T[("xot", ob % 2)] = tr1
            tr1s.append(tr1)
        T[("aost", 0)] = tokm
        T[("r1d", st)] = tr1s
        if st > 0:
            post_st(st - 1)
    post_st(NOT - 1)
    cmb_done = [T[("cmb", st)] for st in range(NOT)]

    ttr = []
    for st in range(NOT):
        pb = pbank[3 + st // 4]
        ttr.append(pr.op("te", lambda e, st=st, pb=pb: e.transpose(
            pb.ap()[0:8, (st % 4) * 128:(st % 4 + 1) * 128],
            combine.ap()[:, st, :], identf.ap()[:, :]),
            waits=cmb_done + [tidf]))
    tev1 = pr.op("ve", lambda e: e.tensor_copy(
        cmbT.ap()[:, 0:512], pbank[3].ap()[0:8, 0:512]), waits=ttr)
    tev2 = pr.op("ve", lambda e: e.tensor_copy(
        cmbT.ap()[:, 512:1024], pbank[4].ap()[0:8, 0:512]), waits=ttr)
    tsg = pr.op("se", lambda e: e.sign(flgt.ap()[:, :], cmbT.ap()[:, :]),
                waits=[tev1, tev2])
    tscn = pr.op("ve", lambda e: e.tensor_tensor_scan(
        post.ap()[:, :], flgt.ap()[:, :], zzt.ap()[:, :], 0.0, ALU.add, ALU.bypass),
        waits=[tsg, tzz])
    tex1 = pr.op("ve", lambda e: e.tensor_tensor(
        post.ap()[:, :], post.ap()[:, :], flgt.ap()[:, :], ALU.subtract),
        waits=[tscn])
    td0 = pr.op("ve", lambda e: e.tensor_scalar(
        d0t.ap()[:, :], post.ap()[:, :], -BIGV, None, ALU.add), waits=[tex1])
    td0 = pr.op("ve", lambda e: e.tensor_tensor(
        d0t.ap()[:, :], d0t.ap()[:, :], flgt.ap()[:, :], ALU.mult), waits=[td0])
    td0 = pr.op("ve", lambda e: e.tensor_scalar(
        d0t.ap()[:, :], d0t.ap()[:, :], BIGV, None, ALU.add), waits=[td0])
    ttd = []
    for st in range(NOT):
        ttd.append(pr.op("te", lambda e, st=st: e.transpose(
            pbank[5].ap()[:, 64 + st * NEXP:64 + (st + 1) * NEXP],
            d0t.ap()[:, st * 128:(st + 1) * 128], identf.ap()[0:8, 0:NEXP]),
            waits=[td0, tev1, tev2]))
    tevd = pr.op("ve", lambda e: e.tensor_copy(
        dTM.ap()[:, :, :], pbank[5].ap()[:, 64:64 + NOT * NEXP]), waits=ttd)
    tprev = tevd
    for st in range(NOT):
        tprev = pr.op("ve", lambda e, st=st: e.tensor_scalar(
            dTM.ap()[:, st, :], dTM.ap()[:, st, :], float(CAP + st), None,
            ALU.min), waits=[tprev])
    for st in range(NOT):
        tprev = pr.op("ve", lambda e, st=st: e.tensor_tensor(
            dTM.ap()[:, st, :], dTM.ap()[:, st, :], erowb.ap()[:, :], ALU.add),
            waits=[tprev, ter])
    tcast = pr.op("ve", lambda e: e.tensor_copy(
        dTMi.ap()[:, :, :], dTM.ap()[:, :, :]), waits=[tprev])
    tpl = []
    for st in range(NOT):
        tpl.append(pr.op("ve", lambda e, st=st: e.tensor_copy(
            payl.ap()[:, st, :, 0],
            tokid.ap()[:, st:st + 1].broadcast_to([128, NEXP])), waits=[ttk]))
        tpl.append(pr.op("ve", lambda e, st=st: e.tensor_copy(
            payl.ap()[:, st, :, 1], combine.ap()[:, st, :]), waits=cmb_done))
    first_w = [tin2, tcast] + tpl
    esc = {}

    def emit_disp(e_):
        toks = []
        for st in range(NOT):
            toks.append(pr.op("gp", lambda e, e_=e_, st=st: e.indirect_dma_start(
                out=idcw_d.rearrange("e c two -> (e c) two"),
                out_offset=IndirectOffsetOnAxis(dTMi.ap()[:, st, e_:e_ + 1], 0),
                in_=payl.ap()[:, st, e_, :], in_offset=None), waits=first_w))
        esc[e_] = toks


    pr.phase("drain_b")
    drain_dma()
    pr.phase("ffn_alloc")
    freeh("woth", "wotl", "aosth0", "aostl0", "xot0", "xot1",
          "hstage", "hfst_hi", "hfst_lo", "r1p0", "r1p1",
          "cmbT", "flgt", "post", "dTM",
          "tokid_s", "erowb", "initt", "prb", "prb2", "prb3", "cmx", "cm2",
          "csum", "cmx8", "identf")
    idcw2 = sb("idcw2", [128, 2 * NSLT, 2])
    idxi = sb("idxi", [128, 2 * NSLT], I32)
    gtile = [sb(f"gtile{i}", [128, E]) for i in range(2)]
    hfme_hi = [sb(f"hfme_hi{i}", [128, NKC, CAP], BF16) for i in range(2)]
    hfme_lo = [sb(f"hfme_lo{i}", [128, NKC, CAP], BF16) for i in range(2)]
    afh = sb("afh", [128, FCH, CAP], BF16)
    afl = sb("afl", [128, FCH, CAP], BF16)
    yslot = sb("yslot", [128, NSLT, E])
    wgf = [sb(f"wgf{i}", [128, NKC, 128]) for i in range(2)]
    wgh = [sb(f"wgh{i}", [128, NKC, 128], BF16) for i in range(2)]
    wgl = [sb(f"wgl{i}", [128, NKC, 128], BF16) for i in range(2)]
    wuf = [sb(f"wuf{i}", [128, NKC, 128]) for i in range(2)]
    wuh = [sb(f"wuh{i}", [128, NKC, 128], BF16) for i in range(2)]
    wul = [sb(f"wul{i}", [128, NKC, 128], BF16) for i in range(2)]
    wdf = [sb(f"wdf{i}", [128, 1024]) for i in range(2)]
    wdh = [sb(f"wdh{i}", [128, 1024], BF16) for i in range(2)]
    wdl = [sb(f"wdl{i}", [128, 1024], BF16) for i in range(2)]
    silu_t = sb("silu_t", [128, CAP])
    a_t = silu_t

    pr.phase("ffn")
    tmz = pr.op("ve", lambda e: e.memset(stg.ap()[:, :], 0.0))
    zt = []
    for st in range(NOT):
        zt.append(pr.op("sy", lambda e, st=st: e.dma_start(
            out=y_d[st * 128:(st + 1) * 128, :], in_=stg.ap()[:, :]), waits=[tmz]))
    for dr in range(NSLT):
        zt.append(pr.op("sy", lambda e, dr=dr: e.dma_start(
            out=y_d[1024 + dr:1025 + dr, :], in_=stg.ap()[0:1, :]), waits=[tmz]))
        zt.append(pr.op("sy", lambda e, dr=dr: e.dma_start(
            out=h2nd[1024 + dr:1025 + dr, :], in_=stg.ap()[0:1, :]), waits=[tmz]))
    def emit_gather(e_):
        ep = e_ % 2
        for t3 in range(NSLT):
            sl = ep * NSLT + t3
            til = pr.op("gp", lambda e, e_=e_, t3=t3, sl=sl: e.dma_start(
                out=idcw2.ap()[:, sl, :],
                in_=idcw_d[e_, t3 * 128:(t3 + 1) * 128, :]),
                waits=esc[e_] + first_w +
                [T.get(("idxfw", sl)), T.get(("cwfw", sl))])
            T[("cwld", sl)] = til
            tci = pr.op("ve", lambda e, sl=sl: e.tensor_copy(
                idxi.ap()[:, sl:sl + 1], idcw2.ap()[:, sl, 0:1]),
                waits=[til, T.get(("idxiw", sl))])
            T[("idxfw", sl)] = tci
            tg = pr.op("gp", lambda e, sl=sl, t3=t3: e.indirect_dma_start(
                out=gtile[t3 % 2].ap()[:, :], out_offset=None,
                in_=h2nd, in_offset=IndirectOffsetOnAxis(idxi.ap()[:, sl:sl + 1], 0)),
                waits=[tci, zt[-1], T.get(("gtw", t3 % 2))])
            th, tl = split3(gtile[t3 % 2].ap()[:, :], shi.ap()[:, :], slo.ap()[:, :],
                            [tg, T.get(("shilo",))])
            T[("gtw", t3 % 2)] = tl

            def spec_g(half, hl, t3=t3, ep=ep):
                dt_ = hfme_hi[ep] if hl == 0 else hfme_lo[ep]
                return dt_.ap()[:, half * 8:(half + 1) * 8, t3 * 128:(t3 + 1) * 128]
            tev = transpose16(shi, slo, spec_g, [th, tl, T.get(("hfmew", ep))])
            T[("shilo",)] = tev
            T[("hfld", e_, t3)] = tev

    emit_disp(0)
    emit_gather(0)
    for e_ in range(NEXP):
        ep = e_ % 2
        hfme_done = [T[("hfld", e_, t3)] for t3 in range(NSLT)]
        for fc in range(FCH):
            bi = fc % 2
            wg_toks = []
            for cq in range(4):
                wg_toks.append(pr.op("sy", lambda e, e_=e_, fc=fc, cq=cq, bi=bi:
                               e.dma_start(
                    out=wgf[bi].ap()[:, 4 * cq:4 * (cq + 1), :],
                    in_=w_gate[e_, 512 * cq:512 * (cq + 1), fc * 128:(fc + 1) * 128]
                    .rearrange("(c p) f -> p c f", p=128)),
                    waits=[T.get(("wgfw", bi))]))
            th = pr.op("se", lambda e, bi=bi: e.copy(
                wgh[bi].ap()[:, :, :], wgf[bi].ap()[:, :, :]),
                waits=wg_toks + [T.get(("wgs", bi))])
            tl = pr.op("ve", lambda e, bi=bi: e.tensor_tensor(
                wgl[bi].ap()[:, :, :], wgf[bi].ap()[:, :, :], wgh[bi].ap()[:, :, :],
                ALU.subtract), waits=wg_toks + [th])
            wg_done = tl
            T[("wgfw", bi)] = tl
            wu_toks = []
            for cq in range(4):
                wu_toks.append(pr.op("sy", lambda e, e_=e_, fc=fc, cq=cq, bi=bi:
                               e.dma_start(
                    out=wuf[bi].ap()[:, 4 * cq:4 * (cq + 1), :],
                    in_=w_up[e_, 512 * cq:512 * (cq + 1), fc * 128:(fc + 1) * 128]
                    .rearrange("(c p) f -> p c f", p=128)),
                    waits=[T.get(("wufw", bi))]))
            th2 = pr.op("se", lambda e, bi=bi: e.copy(
                wuh[bi].ap()[:, :, :], wuf[bi].ap()[:, :, :]),
                waits=wu_toks + [T.get(("wus", bi))])
            tl2 = pr.op("ve", lambda e, bi=bi: e.tensor_tensor(
                wul[bi].ap()[:, :, :], wuf[bi].ap()[:, :, :], wuh[bi].ap()[:, :, :],
                ALU.subtract), waits=wu_toks + [th2])
            T[("wufw", bi)] = tl2
            T[("wgsplit",)] = th2
            tokg = toku = None
            for c in range(NKC):
                tokg = mm3(pbank[0].ap()[:, 0:CAP],
                           wgh[bi].ap()[:, c, :], wgl[bi].ap()[:, c, :],
                           hfme_hi[ep].ap()[:, c, :], hfme_lo[ep].ap()[:, c, :],
                           start=(c == 0), stop=(c == NKC - 1),
                           waits=[wg_done, th] + hfme_done + [T.get(("gu0",))])
            for c in range(NKC):
                toku = mm3(pbank[1].ap()[:, 0:CAP],
                           wuh[bi].ap()[:, c, :], wul[bi].ap()[:, c, :],
                           hfme_hi[ep].ap()[:, c, :], hfme_lo[ep].ap()[:, c, :],
                           start=(c == 0), stop=(c == NKC - 1),
                           waits=[tl2, th2, T.get(("gu1",))])
            T[("wgs", bi)] = tokg
            T[("wus", bi)] = toku
            T[("hfmew", ep)] = toku
            tsl = pr.op("se", lambda e: e.activation(
                silu_t.ap()[:, :], pbank[0].ap()[:, 0:CAP], AF.Silu),
                waits=[tokg, T.get(("silu",)), T.get(("a_t",))])
            T[("gu0",)] = tsl
            tmu_ = pr.op("ve", lambda e: e.tensor_tensor(
                a_t.ap()[:, :], silu_t.ap()[:, :], pbank[1].ap()[:, 0:CAP],
                ALU.mult), waits=[tsl, toku])
            T[("gu1",)] = tmu_
            T[("silu",)] = tmu_
            th3, tl3 = split3(a_t.ap()[:, :], afh.ap()[:, fc, :],
                              afl.ap()[:, fc, :], [tmu_, T.get(("af", fc))])
            T[("a_t",)] = tl3
            T[("af", fc)] = tl3
        a_done = [T[("af", fc)] for fc in range(FCH)]
        evac_toks = {t3: [] for t3 in range(NSLT)}
        for obp in range(2):
            tokm_d = {}
            last_mm = None
            for fc in range(FCH):
                bi = fc % 2
                tok_wd = pr.op("sy", lambda e, e_=e_, fc=fc, obp=obp, bi=bi:
                               e.dma_start(
                    out=wdf[bi].ap()[:, :],
                    in_=w_down[e_, fc * 128:(fc + 1) * 128,
                               obp * 1024:(obp + 1) * 1024]),
                    waits=[T.get(("wdfw", bi))])
                th = pr.op("se", lambda e, bi=bi: e.copy(
                    wdh[bi].ap()[:, :], wdf[bi].ap()[:, :]),
                    waits=[tok_wd, T.get(("wds", bi))])
                tl = pr.op("ve", lambda e, bi=bi: e.tensor_tensor(
                    wdl[bi].ap()[:, :], wdf[bi].ap()[:, :], wdh[bi].ap()[:, :],
                    ALU.subtract), waits=[tok_wd, th])
                T[("wdfw", bi)] = tl
                for t3 in range(NSLT):
                    for ob in range(2):
                        pi = t3 * 2 + ob
                        last_mm = mm3(pbank[pi].ap()[:, 0:512],
                                      afh.ap()[:, fc, t3 * 128:(t3 + 1) * 128],
                                      afl.ap()[:, fc, t3 * 128:(t3 + 1) * 128],
                                      wdh[bi].ap()[:, ob * 512:(ob + 1) * 512],
                                      wdl[bi].ap()[:, ob * 512:(ob + 1) * 512],
                                      start=(fc == 0), stop=(fc == FCH - 1),
                                      waits=a_done + [tl, th] +
                                      ([T.get(("dps", pi))] if fc == 0 else []),
                                      force_tok=(t3 == NSLT - 1 and ob == 1))
                        if fc == FCH - 1:
                            tokm_d[pi] = last_mm
                T[("wds", bi)] = last_mm
            for t3 in range(NSLT):
                for ob in range(2):
                    pi = t3 * 2 + ob
                    dst = yslot.ap()[:, t3, (obp * 1024 + ob * 512):
                                     (obp * 1024 + (ob + 1) * 512)]
                    tev = pr.op("ve", lambda e, pi=pi, dst=dst, t3=t3, ep=ep:
                                e.tensor_scalar(
                                    dst, pbank[pi].ap()[:, 0:512],
                                    idcw2.ap()[:, ep * NSLT + t3, 1:2],
                                    None, ALU.mult),
                                waits=[tokm_d[pi], T.get(("cwld", ep * NSLT + t3)),
                                       T.get(("ysl", t3))])
                    T[("dps", pi)] = tev
                    evac_toks[t3].append(tev)
        if e_ + 1 < NEXP:
            emit_disp(e_ + 1)
            emit_gather(e_ + 1)
        escs = []
        for t3 in range(NSLT):
            sl = ep * NSLT + t3
            ysc = pr.op("gp", lambda e, sl=sl, t3=t3: e.indirect_dma_start(
                out=y_d, out_offset=IndirectOffsetOnAxis(idxi.ap()[:, sl:sl + 1], 0),
                in_=yslot.ap()[:, t3, :], in_offset=None, compute_op=ALU.add),
                waits=evac_toks[t3] + zt + T.get(("ysc",), []))
            escs.append(ysc)
            T[("ysl", t3)] = ysc
            T[("idxiw", sl)] = [ysc]
            T[("cwfw", sl)] = evac_toks[t3][-1]
        T[("ysc",)] = escs

    pr.phase("drain_c")
    drain_dma()
    pr.phase("ln2")
    h2t = [wgf[0].ap().rearrange("p c k -> p (c k)"),
           wgf[1].ap().rearrange("p c k -> p (c k)")]
    for st in range(NOT):
        bi = st % 2
        ty = pr.op("sy", lambda e, st=st, bi=bi: e.dma_start(
            out=gtile[bi].ap()[:, :], in_=y_d[st * 128:(st + 1) * 128, :]),
            waits=[T.get(("l2y", bi))])
        th2 = pr.op("sy", lambda e, st=st, bi=bi: e.dma_start(
            out=h2t[bi][:, :], in_=h2nd[st * 128:(st + 1) * 128, :]),
            waits=[T.get(("l2h", bi))])
        tr2 = pr.op("ve", lambda e, bi=bi: e.tensor_tensor(
            gtile[bi].ap()[:, :], gtile[bi].ap()[:, :], h2t[bi][:, :],
            ALU.add), waits=[ty, th2])
        tq = pr.op("se", lambda e, bi=bi: e.activation(
            h2t[bi][:, :], gtile[bi].ap()[:, :], AF.Square,
            accum_out=lnss.ap()[:, 0:1]), waits=[tr2, T.get(("lnx",))])
        T[("l2h", bi)] = tq
        tsm = pr.op("ve", lambda e, bi=bi: e.reduce_sum(
            lnsm.ap()[:, 0:1], gtile[bi].ap()[:, :], axis=AX.X),
            waits=[tr2, T.get(("lnx",))])
        tmu_ = pr.op("ve", lambda e: e.tensor_scalar(
            lnsm.ap()[:, 0:1], lnsm.ap()[:, 0:1], 1.0 / E, None, ALU.mult),
            waits=[tsm])
        tms = pr.op("ve", lambda e: e.tensor_scalar(
            lnss.ap()[:, 0:1], lnss.ap()[:, 0:1], 1.0 / E, LN_EPS,
            ALU.mult, ALU.add), waits=[tq])
        tm2 = pr.op("ve", lambda e: e.tensor_tensor(
            lnt.ap()[:, 0:1], lnsm.ap()[:, 0:1], lnsm.ap()[:, 0:1], ALU.mult),
            waits=[tmu_])
        tvr = pr.op("ve", lambda e: e.tensor_tensor(
            lnt.ap()[:, 0:1], lnss.ap()[:, 0:1], lnt.ap()[:, 0:1],
            ALU.subtract), waits=[tms, tm2])
        trc = pr.op("ve", lambda e: e.reciprocal(
            lnt.ap()[:, 0:1], lnt.ap()[:, 0:1]), waits=[tvr])
        tsq = pr.op("se", lambda e: e.activation(
            lnt.ap()[:, 0:1], lnt.ap()[:, 0:1], AF.Sqrt), waits=[trc])
        T[("lnx",)] = tsq
        tou = pr.op("ve", lambda e, bi=bi: e.tensor_scalar(
            stg.ap()[:, :], gtile[bi].ap()[:, :], lnsm.ap()[:, 0:1],
            lnt.ap()[:, 0:1], ALU.subtract, ALU.mult),
            waits=[tsq, tr2, T.get(("ostg",))])
        tst = pr.op("sy", lambda e, st=st: e.dma_start(
            out=out_own[st * 128:(st + 1) * 128, :], in_=stg.ap()[:, :]),
            waits=[tou])
        T[("ostg",)] = tst
        T[("l2y", bi)] = tou
        T[("store", st)] = tst

    pr.phase("fin")
    pr.op("sy", lambda e: e.nop(),
          waits=[T[("store", st)] for st in range(NOT)], inc=False)

    pr.emit()
    return nc


_NC_CACHE = {}


def _get_nc():
    if "nc" not in _NC_CACHE:
        _NC_CACHE["nc"] = build_program()
    return _NC_CACHE["nc"]


def make_in_maps(x, common):
    in_maps = []
    for c in range(8):
        b, j = c // 4, c % 4
        cos_k, sin_k, cos_q, sin_q, masks = _host_tables(j)
        m = dict(common)
        m["x_own"] = np.ascontiguousarray(x[b, j::4, :])
        m["x_keys"] = np.ascontiguousarray(x[b])
        m["masks_in"] = masks
        m["cos_k"] = cos_k
        m["sin_k"] = sin_k
        m["cos_q"] = cos_q
        m["sin_q"] = sin_q
        in_maps.append(m)
    return in_maps


def _moe_init():
    init = np.zeros((NEXP, CAP + 16, 2), np.float32)
    init[:, :, 0] = DUMPT
    for t3 in range(NSLT):
        init[:, t3 * 128:(t3 + 1) * 128, 0] = DUMPT + t3
    return init


def make_common(L0_attn_Wq, L0_attn_Wk, L0_attn_Wv, L0_attn_Wo,
                L0_attn_q_norm_w, L0_attn_k_norm_w, L0_attn_Wrq, L0_attn_Wrk,
                L0_attn_graph_scale, L0_ffn_gate_w, L0_ffn_w_gate, L0_ffn_w_up,
                L0_ffn_w_down):
    cc = np.ascontiguousarray
    return {
        "Wq": cc(np.asarray(L0_attn_Wq, np.float32).T),
        "Wk": cc(np.asarray(L0_attn_Wk, np.float32).T),
        "Wv": cc(np.asarray(L0_attn_Wv, np.float32).T),
        "Wo": cc(np.asarray(L0_attn_Wo, np.float32).T),
        "Wrq": cc(np.asarray(L0_attn_Wrq, np.float32).T),
        "Wrk": cc(np.asarray(L0_attn_Wrk, np.float32).T),
        "qnw": cc(np.asarray(L0_attn_q_norm_w, np.float32)),
        "knw": cc(np.asarray(L0_attn_k_norm_w, np.float32)),
        "gsc": cc(np.asarray(L0_attn_graph_scale, np.float32)),
        "gate_w": cc(np.asarray(L0_ffn_gate_w, np.float32)),
        "w_gate": cc(np.asarray(L0_ffn_w_gate, np.float32)),
        "w_up": cc(np.asarray(L0_ffn_w_up, np.float32)),
        "w_down": cc(np.asarray(L0_ffn_w_down, np.float32)),
        "ideye": np.eye(128, dtype=BF),
        "moe_init": _moe_init(),
        "erow": (np.arange(NEXP) * (CAP + 16)).astype(np.float32),
        "tokid_c": np.arange(1024, dtype=np.float32),
    }


def kernel(L0_attn_x, L0_attn_Wq, L0_attn_Wk, L0_attn_Wv, L0_attn_Wo,
           L0_attn_q_norm_w, L0_attn_k_norm_w, L0_attn_Wrq, L0_attn_Wrk,
           L0_attn_graph_scale, L0_ffn_gate_w, L0_ffn_w_gate, L0_ffn_w_up,
           L0_ffn_w_down):
    x = np.ascontiguousarray(np.asarray(L0_attn_x, dtype=np.float32))
    common = make_common(L0_attn_Wq, L0_attn_Wk, L0_attn_Wv, L0_attn_Wo,
                         L0_attn_q_norm_w, L0_attn_k_norm_w, L0_attn_Wrq,
                         L0_attn_Wrk, L0_attn_graph_scale, L0_ffn_gate_w,
                         L0_ffn_w_gate, L0_ffn_w_up, L0_ffn_w_down)
    in_maps = make_in_maps(x, common)
    nc = _get_nc()
    res = run_bass_kernel_spmd(nc, in_maps, core_ids=list(range(8)))
    out = np.empty((B, S, E), np.float32)
    for c in range(8):
        b, j = c // 4, c % 4
        out[b, j::4, :] = res.results[c]["out_own"]
    return out


# version 20
# speedup vs baseline: 1.0768x; 1.0768x over previous
import numpy as np
import ml_dtypes
import concourse.bass as bass
import concourse.mybir as mybir
from concourse.bass_utils import run_bass_kernel_spmd
from concourse.bass_core_rust import IndirectOffsetOnAxis

F32 = mybir.dt.float32
BF16 = mybir.dt.bfloat16
AF = mybir.ActivationFunctionType
ALU = mybir.AluOpType
AX = mybir.AxisListType
BF = ml_dtypes.bfloat16

B, S, E = 2, 4096, 2048
HQ, KVH, D = 16, 4, 128
HG, RNK, SYM = 12, 32, 4
FF, NEXP = 1408, 8
NKT = S // 128
NOT = 8
NKC = 16
EXP_SHIFT = -12.0
RMS_EPS = 1e-6
LN_EPS = 1e-5
FCH = FF // 128
I32 = mybir.dt.int32
CAP = 384
NSLT = CAP // 128
BIGV = 100000.0
DUMPT = 1024


def _flat(waits):
    out = []
    for t in waits:
        if t is None:
            continue
        if isinstance(t, list):
            out.extend(x for x in t if x is not None)
        else:
            out.append(t)
    return out


class Prog:
    ENGS = ("te", "ve", "se", "gp", "sy")
    NPOOL = {"sy": 12, "gp": 8}

    def __init__(self, nc):
        self.nc = nc
        self.sem = {}
        self.cnt = {}
        for n in ("te", "ve", "se"):
            self.sem[n] = nc.alloc_semaphore("s_" + n)
            self.cnt[n] = 0
        self.pool = {}
        for q, k in self.NPOOL.items():
            self.pool[q] = []
            for i in range(k):
                nm = f"{q}{i}"
                self.sem[nm] = nc.alloc_semaphore("s_" + nm)
                self.cnt[nm] = 0
                self.pool[q].append(nm)
        self.rr = {q: 0 for q in self.NPOOL}
        self.phases = []
        self.cur = None
        self.waited = None

    def phase(self, name=""):
        self.cur = {e: [] for e in self.ENGS}
        self.waited = {e: {} for e in self.ENGS}
        self.phases.append(self.cur)

    def op(self, eng, fn, waits=(), inc=True):
        w = []
        for (s, v) in _flat(waits):
            if self.waited[eng].get(s, 0) < v:
                self.waited[eng][s] = v
                w.append((s, v))
        tok = None
        if eng in ("sy", "gp"):
            if inc:
                nm = self.pool[eng][self.rr[eng] % len(self.pool[eng])]
                self.rr[eng] += 1
                if self.cnt[nm] > 0 and self.waited[eng].get(nm, 0) < self.cnt[nm]:
                    w.append((nm, self.cnt[nm]))
                    self.waited[eng][nm] = self.cnt[nm]
                self.cnt[nm] += 16
                tok = (nm, self.cnt[nm])
                self.cur[eng].append((w, fn, nm, 16))
            else:
                self.cur[eng].append((w, fn, None, 16))
            return tok
        if inc:
            self.cnt[eng] += 1
            tok = (eng, self.cnt[eng])
        self.cur[eng].append((w, fn, eng if inc else None, 1))
        return tok

    def tok(self, eng):
        return (eng, self.cnt[eng]) if self.cnt.get(eng) else None

    def emit(self):
        nc = self.nc
        for ph in self.phases:
            if not any(ops for ops in ph.values()):
                continue
            with nc.Block() as blk:
                for ename in self.ENGS:
                    ops = ph[ename]
                    if not ops:
                        continue
                    sec = {"te": blk.tensor, "ve": blk.vector, "se": blk.scalar,
                           "gp": blk.gpsimd, "sy": blk.sync}[ename]

                    def body(e, ops=ops, ename=ename):
                        for (w, fn, inc, n) in ops:
                            for (s, v) in w:
                                e.wait_ge(self.sem[s], v)
                            inst = fn(e)
                            if inc is not None:
                                inst.then_inc(self.sem[inc], n)
                    sec(body)


def _host_tables(j):
    inv_freq = 1.0 / 10000.0 ** (np.arange(0, D, 2, dtype=np.float64) / D)
    tpos = np.arange(S, dtype=np.float64)
    freqs = np.outer(tpos, inv_freq)
    emb = np.concatenate([freqs, freqs], -1)
    cos = np.cos(emb).astype(np.float32)
    sin = np.sin(emb).astype(np.float32)
    cos_k = np.ascontiguousarray(cos.reshape(NKT, 128, D))
    sin_k = np.ascontiguousarray(sin.reshape(NKT, 128, D))
    own = 4 * np.arange(1024) + j
    cos_q = np.ascontiguousarray(cos[own].reshape(NOT, 128, D))
    sin_q = np.ascontiguousarray(sin[own].reshape(NOT, 128, D))
    t = np.arange(128)[:, None]
    s4 = 4 * np.arange(512)[None, :]
    masks = np.ascontiguousarray(
        np.stack([(t - s4 <= j - 128 * d) for d in range(16)]).astype(BF))
    return cos_k, sin_k, cos_q, sin_q, masks


def build_program():
    nc = bass.Bass("TRN2", target_bir_lowering=False)
    pr = Prog(nc)

    def din(name, shape, dtype=F32):
        return nc.dram_tensor(name, shape, dtype, kind="ExternalInput").ap()

    x_own = din("x_own", [1024, E])
    x_keys = din("x_keys", [S, E])
    Wq = din("Wq", [E, E])
    Wk = din("Wk", [E, KVH * D])
    Wv = din("Wv", [E, KVH * D])
    Wo = din("Wo", [E, E])
    Wrq = din("Wrq", [E, HG * RNK])
    Wrk = din("Wrk", [E, HG * RNK])
    qnw = din("qnw", [D])
    knw = din("knw", [D])
    gsc = din("gsc", [HG])
    gate_w = din("gate_w", [E, NEXP])
    w_gate = din("w_gate", [NEXP, E, FF])
    w_up = din("w_up", [NEXP, E, FF])
    w_down = din("w_down", [NEXP, FF, E])
    ideye = din("ideye", [128, 128], BF16)
    masks_in = din("masks_in", [16, 128, 512], BF16)
    cosk_in = din("cos_k", [NKT, 128, D])
    sink_in = din("sin_k", [NKT, 128, D])
    cosq_in = din("cos_q", [NOT, 128, D])
    sinq_in = din("sin_q", [NOT, 128, D])
    moe_init = din("moe_init", [NEXP, CAP + 16, 2])
    erow_in = din("erow", [NEXP])
    tokid_in = din("tokid_c", [1024])
    out_own = nc.dram_tensor("out_own", [1024, E], F32, kind="ExternalOutput").ap()
    aod_hi = nc.dram_tensor("aod_hi", [NOT, NKC, 128, 128], BF16).ap()
    aod_lo = nc.dram_tensor("aod_lo", [NOT, NKC, 128, 128], BF16).ap()
    xkf_hi = nc.dram_tensor("xkf_hi", [NKT, NKC, 128, 128], BF16).ap()
    xkf_lo = nc.dram_tensor("xkf_lo", [NKT, NKC, 128, 128], BF16).ap()
    xof_hi = nc.dram_tensor("xof_hi", [NOT, NKC, 128, 128], BF16).ap()
    xof_lo = nc.dram_tensor("xof_lo", [NOT, NKC, 128, 128], BF16).ap()
    h2nd = nc.dram_tensor("h2nd", [1024 + NSLT, E], F32).ap()
    idcw_d = nc.dram_tensor("idcw_d", [NEXP, CAP + 16, 2], F32).ap()
    y_d = nc.dram_tensor("y_d", [1024 + NSLT, E], F32).ap()

    handles = {}

    alloc_seq = []

    def sb(name, shape, dtype=F32):
        cm = nc.sbuf_tensor(name, shape, dtype)
        h = cm.__enter__()
        handles[name] = cm
        alloc_seq.append(name)
        return h

    def freeh(*names):
        for n in sorted(names, key=lambda n: -alloc_seq.index(n)):
            handles.pop(n).__exit__(None, None, None)

    ident = sb("ident", [128, 128], BF16)
    onesc = sb("onesc", [128, 1], BF16)
    onesf = sb("onesf", [128, 1])
    shiftt = sb("shiftt", [128, 1])
    onesr = sb("onesr", [1, 128])
    gwhi = sb("gwhi", [128, NKC, NEXP], BF16)
    gwlo = sb("gwlo", [128, NKC, NEXP], BF16)
    gwf = sb("gwf", [128, NKC, NEXP])
    combine = sb("combine", [128, NOT, NEXP])
    mu1 = sb("mu1", [128, NOT])
    s1v = sb("s1v", [128, NOT])
    gsvec = sb("gsvec", [128, 1])

    stg = sb("stg", [128, E])
    shi = sb("shi", [128, E], BF16)
    slo = sb("slo", [128, E], BF16)
    qnb = sb("qnb", [128, 128])
    knb = sb("knb", [128, 128])
    xfh = [sb("xfh0", [128, NKC, 256], BF16)] * 2
    xfl = [sb("xfl0", [128, NKC, 256], BF16)] * 2
    wt_hi = sb("wt_hi", [128, NKC, 512], BF16)
    wt_lo = sb("wt_lo", [128, NKC, 512], BF16)
    masks = sb("masks", [128, 16, 512], BF16)
    k_hi = sb("k_hi", [128, NKT, 128], BF16)
    k_lo = sb("k_lo", [128, NKT, 128], BF16)
    v_hi = sb("v_hi", [128, NKT, 128], BF16)
    v_lo = sb("v_lo", [128, NKT, 128], BF16)
    rks_hi = sb("rks_hi", [128, NKT, 128], BF16)
    rks_lo = sb("rks_lo", [128, NKT, 128], BF16)
    rqs_hi = sb("rqs_hi", [128, NKT, 128], BF16)
    rqs_lo = sb("rqs_lo", [128, NKT, 128], BF16)
    q_hi = sb("q_hi", [128, 4, NOT, 128], BF16)
    q_lo = sb("q_lo", [128, 4, NOT, 128], BF16)
    rqo_hi = sb("rqo_hi", [128, NOT, 128], BF16)
    rqo_lo = sb("rqo_lo", [128, NOT, 128], BF16)
    rko_hi = sb("rko_hi", [128, NOT, 128], BF16)
    rko_lo = sb("rko_lo", [128, NOT, 128], BF16)
    p_f = [sb(f"p_f{i}", [128, 512]) for i in range(2)]
    p_hi = [sb(f"p_hi{i}", [128, 512], BF16) for i in range(2)]
    p_lo = [sb(f"p_lo{i}", [128, 512], BF16) for i in range(2)]
    lrow = sb("lrow", [1, 512])
    lrec = sb("lrec", [1, 512])
    rbc = sb("rbc", [128, 512])
    aos = sb("aos", [128, 512])
    aosh = sb("aosh", [128, 512], BF16)
    aosl = sb("aosl", [128, 512], BF16)
    cost = [sb(f"cost{i}", [128, 128]) for i in range(2)]
    sint = [sb(f"sint{i}", [128, 128]) for i in range(2)]
    ropw = sb("ropw", [128, 3, 512])
    ssq = sb("ssq", [128, 8])
    rstd = sb("rstd", [128, 8])
    tmp1 = sb("tmp1", [128, 8])
    sqs = sb("sqs", [128, 128], BF16)
    sthi = sb("sthi", [128, 2, 128], BF16)
    stlo = sb("stlo", [128, 2, 128], BF16)

    pbank = [nc.alloc_psum_tensor(f"pb{i}", [128, 512], F32) for i in range(6)]
    tbank = [nc.alloc_psum_tensor(f"tb{i}", [128, 1024], BF16) for i in range(2)]

    T = {}

    pr.phase("consts")
    c0 = []
    c0.append(pr.op("sy", lambda e: e.dma_start(out=ident.ap()[:, :], in_=ideye)))
    for d8 in range(16):
        c0.append(pr.op("sy", lambda e, d8=d8: e.dma_start(
            out=masks.ap()[:, d8, :], in_=masks_in[d8, :, :])))
    c0.append(pr.op("sy", lambda e: e.dma_start(
        out=qnb.ap()[:, :], in_=qnw[None, :].broadcast_to([128, D]))))
    c0.append(pr.op("sy", lambda e: e.dma_start(
        out=knb.ap()[:, :], in_=knw[None, :].broadcast_to([128, D]))))
    tok_gwf = []
    for cq in range(4):
        tok_gwf.append(pr.op("sy", lambda e, cq=cq: e.dma_start(
            out=gwf.ap()[:, 4 * cq:4 * (cq + 1), :],
            in_=gate_w[512 * cq:512 * (cq + 1), :]
            .rearrange("(c p) x -> p c x", p=128))))
    c0.append(pr.op("ve", lambda e: e.memset(onesc.ap()[:, :], 1.0)))
    c0.append(pr.op("ve", lambda e: e.memset(onesf.ap()[:, :], 1.0)))
    c0.append(pr.op("ve", lambda e: e.memset(shiftt.ap()[:, :], EXP_SHIFT)))
    c0.append(pr.op("ve", lambda e: e.memset(onesr.ap()[:1, :], 1.0)))
    tgh = pr.op("se", lambda e: e.copy(gwhi.ap()[:, :, :], gwf.ap()[:, :, :]),
                waits=tok_gwf)
    tgl = pr.op("ve", lambda e: e.tensor_tensor(
        gwlo.ap()[:, :, :], gwf.ap()[:, :, :], gwhi.ap()[:, :, :], ALU.subtract),
        waits=tok_gwf + [tgh])
    c0 += [tgh, tgl]
    T["consts"] = c0

    def split3(src_ap, hi_ap, lo_ap, waits, eng_hi="ve"):
        th = pr.op(eng_hi, lambda e: e.tensor_copy(hi_ap, src_ap) if eng_hi == "ve"
                   else e.copy(hi_ap, src_ap), waits=waits)
        tl = pr.op("ve", lambda e: e.tensor_tensor(lo_ap, src_ap, hi_ap, ALU.subtract),
                   waits=_flat(waits) + [th])
        return th, tl

    def mm3(psum_ap, st_h, st_l, mv_h, mv_l, start, stop, waits, tpos=None,
            force_tok=False):
        kw = {} if tpos is None else {"tile_position": tpos}
        pr.op("te", lambda e: e.matmul(psum_ap, st_h, mv_h, start=start, stop=False, **kw),
              waits=waits, inc=False)
        pr.op("te", lambda e: e.matmul(psum_ap, st_h, mv_l, start=False, stop=False, **kw),
              inc=False)
        return pr.op("te", lambda e: e.matmul(psum_ap, st_l, mv_h, start=False, stop=stop, **kw),
                     inc=(stop or force_tok))

    def transpose16(src_hi, src_lo, evac_specs, waits):
        last = None
        for half in range(2):
            tts = []
            for ci in range(8):
                c = half * 8 + ci
                for hl, srct in ((0, src_hi), (1, src_lo)):
                    tts.append(pr.op("te", lambda e, c=c, ci=ci, hl=hl, srct=srct:
                               e.transpose(tbank[hl].ap()[:, ci * 128:(ci + 1) * 128],
                                           srct.ap()[:, c * 128:(c + 1) * 128],
                                           ident.ap()[:, :]),
                               waits=_flat(waits) + [T.get(("tbank", hl))]))
            for hl in range(2):
                dst = evac_specs(half, hl)
                tev = pr.op("ve", lambda e, hl=hl, dst=dst:
                            e.tensor_copy(
                                dst,
                                tbank[hl].ap()[:, :].rearrange("p (c k) -> p c k",
                                                               k=128)),
                            waits=tts)
                last = tev
                T[("tbank", hl)] = tev
        return last

    def load_xf(co, xsh, xsl, tt, war):
        toks = []
        for qc in range(4):
            for (xs, dstt) in ((xsh, xfh[0]), (xsl, xfl[0])):
                toks.append(pr.op("sy", lambda e, qc=qc, xs=xs, dstt=dstt, tt=tt,
                            co=co:
                            e.dma_start(
                                out=dstt.ap()[:, 4 * qc:4 * (qc + 1), co:co + 128],
                                in_=xs[tt, 4 * qc:4 * (qc + 1)]
                                .rearrange("c d k -> d c k")),
                            waits=_flat([war])))
        return toks

    def build_wt(Wsrc, r0, wcol, extra=()):
        stgv = stg.ap().rearrange("p (c o) -> p c o", o=128)
        toks = []
        for cq in range(4):
            toks.append(pr.op("sy", lambda e, cq=cq, Wsrc=Wsrc, r0=r0:
                        e.dma_start(
                out=stgv[:, 4 * cq:4 * (cq + 1), :],
                in_=Wsrc[512 * cq:512 * (cq + 1), r0:r0 + 128]
                .rearrange("(c p) o -> p c o", p=128)),
                waits=[T.get(("stg",))]))
        th, tl = split3(stgv[:, :, :],
                        wt_hi.ap()[:, :, wcol:wcol + 128],
                        wt_lo.ap()[:, :, wcol:wcol + 128],
                        toks + [T.get(("wts",))] + _flat(extra), eng_hi="se")
        T[("stg",)] = tl
        T[("wts",)] = tl
        return tl

    def rope_tm(src_ap, dst_ap, ct, st_, rst_scalar, nb_ap, waits, out_ta=None,
                bp=0):
        kn = ropw.ap()[:, bp, 0:128]
        tt1 = ropw.ap()[:, bp, 128:256]
        ta = pr.op("ve", lambda e: e.scalar_tensor_tensor(
            kn, src_ap, rst_scalar, nb_ap, ALU.mult, ALU.mult), waits=waits)
        if out_ta is not None:
            out_ta.append(ta)
        tb_ = pr.op("ve", lambda e: e.tensor_tensor(tt1, kn, ct, ALU.mult), waits=[ta])
        tu1 = pr.op("ve", lambda e: e.tensor_tensor(
            ropw.ap()[:, bp, 256:320], kn[:, 64:128], st_[:, 0:64], ALU.mult), waits=[ta])
        tu2 = pr.op("ve", lambda e: e.tensor_tensor(
            ropw.ap()[:, bp, 320:384], kn[:, 0:64], st_[:, 64:128], ALU.mult), waits=[ta])
        to1 = pr.op("ve", lambda e: e.tensor_tensor(
            dst_ap[:, 0:64], tt1[:, 0:64], ropw.ap()[:, bp, 256:320], ALU.subtract),
            waits=[tb_, tu1])
        to2 = pr.op("ve", lambda e: e.tensor_tensor(
            dst_ap[:, 64:128], tt1[:, 64:128], ropw.ap()[:, bp, 320:384], ALU.add),
            waits=[tb_, tu2])
        return [to1, to2]


    def drain_dma():
        toks = [(nm, pr.cnt[nm]) for q in ("sy", "gp") for nm in pr.pool[q]
                if pr.cnt[nm]]
        pr.op("sy", lambda e: e.nop(), waits=toks, inc=False)

    pr.phase("xscratch")
    for (xdram, xdsth, xdstl, ntt) in ((x_keys, xkf_hi, xkf_lo, NKT),
                                       (x_own, xof_hi, xof_lo, NOT)):
        for tt in range(ntt):
            co = 128 * (tt % 2)
            tok_d = pr.op("sy", lambda e, tt=tt, xdram=xdram: e.dma_start(
                out=stg.ap()[:, :], in_=xdram[tt * 128:(tt + 1) * 128, :]),
                waits=[T.get(("stg",))])
            th, tl = split3(stg.ap()[:, :], shi.ap()[:, :], slo.ap()[:, :],
                            [tok_d, T.get(("shilo",))], eng_hi="se")
            T[("stg",)] = tl

            def spec(half, hl, co=co):
                dtile = xfh[0] if hl == 0 else xfl[0]
                return dtile.ap()[:, half * 8:(half + 1) * 8, co:co + 128]
            tev = transpose16(shi, slo, spec, [th, tl, T.get(("xsc", co))])
            T[("shilo",)] = tev
            w1 = pr.op("gp", lambda e, tt=tt, co=co, xdsth=xdsth: e.dma_start(
                out=xdsth[tt].rearrange("c d k -> d c k"),
                in_=xfh[0].ap()[:, :, co:co + 128]), waits=[tev])
            w2 = pr.op("gp", lambda e, tt=tt, co=co, xdstl=xdstl: e.dma_start(
                out=xdstl[tt].rearrange("c d k -> d c k"),
                in_=xfl[0].ap()[:, :, co:co + 128]), waits=[tev])
            T[("xsc", co)] = [w1, w2]
    tok_xsc = [(nm, pr.cnt[nm]) for nm in pr.pool["gp"] if pr.cnt[nm]]

    for g in range(4):
        pr.phase(f"prep{g}")
        tok_gs = None
        if g < 3:
            gst = []
            for hq in range(4):
                gst.append(pr.op("sy", lambda e, g=g, hq=hq: e.dma_start(
                    out=gsvec.ap()[32 * hq:32 * (hq + 1), 0:1],
                    in_=gsc[4 * g + hq:4 * g + hq + 1][None, :]
                    .broadcast_to([32, 1])),
                    waits=[T.get(("gsv",))]))
            tok_gs = gst
            if g == 0:
                tok_gs = [pr.op("se", lambda e: e.mul(
                    gsvec.ap()[:, :], gsvec.ap()[:, :], 0.5), waits=gst)]
        war_te = [pr.tok("te")]
        t1 = build_wt(Wk, g * 128, 0, extra=war_te)
        t2 = build_wt(Wv, g * 128, 128, extra=war_te)
        t3 = t4 = None
        if g < 3:
            t3 = build_wt(Wrk, g * 128, 256, extra=war_te)
            t4 = build_wt(Wrq, g * 128, 384, extra=war_te)
        tok_kv = [t1, t2]
        tok_rr = {0: t3, 1: t4}

        for tt in range(NKT):
            co = 128 * (tt % 2)
            hp = tt % 2
            stoks = load_xf(co, xkf_hi, xkf_lo, tt,
                            [T.get(("xf", co))] + tok_xsc)
            ps = pbank[4 + hp]
            tokm = None
            for c in range(NKC):
                tokm = mm3(ps.ap()[:, 0:256],
                           xfh[0].ap()[:, c, co:co + 128],
                           xfl[0].ap()[:, c, co:co + 128],
                           wt_hi.ap()[:, c, 0:256], wt_lo.ap()[:, c, 0:256],
                           start=(c == 0), stop=(c == NKC - 1),
                           waits=stoks + [tok_kv, T.get(("pswar", 4 + hp))])
            tssq = pr.op("se", lambda e, ps=ps, hp=hp: e.activation(
                sqs.ap()[:, 0:128], ps.ap()[:, 0:128], AF.Square,
                accum_out=ssq.ap()[:, hp:hp + 1]), waits=[tokm, T.get(("ssq", hp))])
            tv = pr.op("ve", lambda e, hp=hp: e.tensor_scalar(
                tmp1.ap()[:, hp:hp + 1], ssq.ap()[:, hp:hp + 1], 1.0 / D, RMS_EPS,
                ALU.mult, ALU.add), waits=[tssq])
            tv = pr.op("ve", lambda e, hp=hp: e.reciprocal(
                rstd.ap()[:, hp:hp + 1], tmp1.ap()[:, hp:hp + 1]), waits=[tv])
            tv = pr.op("se", lambda e, hp=hp: e.activation(
                rstd.ap()[:, 4 + hp:5 + hp], rstd.ap()[:, hp:hp + 1], AF.Sqrt),
                waits=[tv])
            T[("ssq", hp)] = tv
            tok_ck = pr.op("sy", lambda e, tt=tt: e.dma_start(
                out=cost[tt % 2].ap()[:, :], in_=cosk_in[tt, :, :]),
                waits=[T.get(("cosk", tt % 2))])
            tok_sk = pr.op("sy", lambda e, tt=tt: e.dma_start(
                out=sint[tt % 2].ap()[:, :], in_=sink_in[tt, :, :]),
                waits=[T.get(("cosk", tt % 2))])
            rt = rope_tm(ps.ap()[:, 0:128], ropw.ap()[:, hp, 384:512],
                         cost[tt % 2].ap()[:, :], sint[tt % 2].ap()[:, :],
                         rstd.ap()[:, 4 + hp:5 + hp], knb.ap()[:, :],
                         [tv, tok_ck, tok_sk, T.get(("ropw", hp))], bp=hp)
            th, tl = split3(ropw.ap()[:, hp, 384:512], sthi.ap()[:, hp, :],
                            stlo.ap()[:, hp, :],
                            rt + [T.get(("stkv", hp))], eng_hi="se")
            ttr = []
            for hl, srct in ((0, sthi), (1, stlo)):
                ttr.append(pr.op("te", lambda e, hl=hl, srct=srct, hp=hp:
                           e.transpose(tbank[0].ap()[:, (2 * hp + hl) * 128:(2 * hp + hl + 1) * 128],
                                       srct.ap()[:, hp, :], ident.ap()[:, :]),
                           waits=[th, tl, T.get(("tbank", 0))]))
            tev_h = pr.op("ve", lambda e, tt=tt, hp=hp: e.tensor_copy(
                k_hi.ap()[:, tt, :], tbank[0].ap()[:, 2 * hp * 128:(2 * hp + 1) * 128]), waits=ttr)
            tev_l = pr.op("ve", lambda e, tt=tt, hp=hp: e.tensor_copy(
                k_lo.ap()[:, tt, :], tbank[0].ap()[:, (2 * hp + 1) * 128:(2 * hp + 2) * 128]), waits=ttr)
            T[("tbank", 0)] = tev_l
            T[("stkv", hp)] = tev_l
            T[("ropw", hp)] = tev_l
            T[("cosk", tt % 2)] = rt[-1]
            tvh = pr.op("se", lambda e, tt=tt, ps=ps: e.copy(
                v_hi.ap()[:, tt, :], ps.ap()[:, 128:256]), waits=[tokm, tssq])
            tvl = pr.op("ve", lambda e, tt=tt, ps=ps: e.tensor_tensor(
                v_lo.ap()[:, tt, :], ps.ap()[:, 128:256], v_hi.ap()[:, tt, :],
                ALU.subtract), waits=[tokm, tvh])
            T[("pswar", 4 + hp)] = [tvl, tssq, rt[-1]]
            if g < 3:
                specs = [(256, rks_hi, rks_lo, pbank[0], 0)]
                if g == 0:
                    specs.append((384, rqs_hi, rqs_lo, pbank[1], 1))
                for (wcol, dsth, dstl, ps2, bk) in specs:
                    tokm2 = None
                    for c in range(NKC):
                        tokm2 = mm3(ps2.ap()[:, 0:128],
                                    wt_hi.ap()[:, c, wcol:wcol + 128],
                                    wt_lo.ap()[:, c, wcol:wcol + 128],
                                    xfh[0].ap()[:, c, co:co + 128],
                                    xfl[0].ap()[:, c, co:co + 128],
                                    start=(c == 0), stop=(c == NKC - 1),
                                    waits=stoks + [tok_rr[bk], T.get(("pswar", bk))])
                    tsc = pr.op("ve", lambda e, ps2=ps2, bk=bk: e.tensor_scalar(
                        ropw.ap()[:, 2, 128 * bk:128 * bk + 128],
                        ps2.ap()[:, 0:128], gsvec.ap()[:, 0:1],
                        None, ALU.mult),
                        waits=[tokm2, T.get(("ropw2", bk))] + _flat([tok_gs]))
                    th2, tl2 = split3(ropw.ap()[:, 2, 128 * bk:128 * bk + 128],
                                      dsth.ap()[:, tt, :], dstl.ap()[:, tt, :],
                                      [tsc], eng_hi="se")
                    T[("pswar", bk)] = tl2
                    T[("ropw2", bk)] = tl2
            T[("xf", co)] = pr.tok("te")

        if g < 3:
            for tt in range(NOT):
                co = 128 * (tt % 2)
                stoks = load_xf(co, xof_hi, xof_lo, tt,
                                [T.get(("xf", co))] + tok_xsc)
                specs = [(384, rqo_hi, rqo_lo, pbank[0], 0)]
                if g == 0:
                    specs.append((256, rko_hi, rko_lo, pbank[1], 1))
                for (wcol, dsth, dstl, ps2, bk) in specs:
                    tokm2 = None
                    for c in range(NKC):
                        tokm2 = mm3(ps2.ap()[:, 0:128],
                                    wt_hi.ap()[:, c, wcol:wcol + 128],
                                    wt_lo.ap()[:, c, wcol:wcol + 128],
                                    xfh[0].ap()[:, c, co:co + 128],
                                    xfl[0].ap()[:, c, co:co + 128],
                                    start=(c == 0), stop=(c == NKC - 1),
                                    waits=stoks + [T.get(("pswar", bk))])
                    th2, tl2 = split3(ps2.ap()[:, 0:128],
                                      dsth.ap()[:, tt, :], dstl.ap()[:, tt, :],
                                      [tokm2], eng_hi="se")
                    T[("pswar", bk)] = tl2
                T[("xf", co)] = pr.tok("te")

        war_te = [pr.tok("te")]
        tok_wq = [build_wt(Wq, 512 * g + 128 * oi, 128 * oi, extra=war_te)
                  for oi in range(4)]
        for tt in range(NOT):
            co = 128 * (tt % 2)
            stoks = load_xf(co, xof_hi, xof_lo, tt,
                            [T.get(("xf", co))] + tok_xsc)
            ps = pbank[4 + (tt % 2)]
            tokm = None
            for c in range(NKC):
                tokm = mm3(ps.ap()[:, 0:512],
                           xfh[0].ap()[:, c, co:co + 128], xfl[0].ap()[:, c, co:co + 128],
                           wt_hi.ap()[:, c, 0:512], wt_lo.ap()[:, c, 0:512],
                           start=(c == 0), stop=(c == NKC - 1),
                           waits=stoks + [tok_wq, T.get(("pswar", 4 + tt % 2))])
            tok_cq = pr.op("sy", lambda e, tt=tt: e.dma_start(
                out=cost[tt % 2].ap()[:, :], in_=cosq_in[tt, :, :]),
                waits=[T.get(("cosk", tt % 2))])
            tok_sq = pr.op("sy", lambda e, tt=tt: e.dma_start(
                out=sint[tt % 2].ap()[:, :], in_=sinq_in[tt, :, :]),
                waits=[T.get(("cosk", tt % 2))])
            last = None
            prev_ta = []
            for h in range(4):
                hp = h % 2
                tssq = pr.op("se", lambda e, ps=ps, h=h, hp=hp: e.activation(
                    sqs.ap()[:, 0:128],
                    ps.ap()[:, h * 128:(h + 1) * 128],
                    AF.Square, accum_out=ssq.ap()[:, hp:hp + 1]),
                    waits=[tokm, T.get(("ssq", hp))] + prev_ta)
                prev_ta = []
                tv = pr.op("ve", lambda e, hp=hp: e.tensor_scalar(
                    tmp1.ap()[:, hp:hp + 1], ssq.ap()[:, hp:hp + 1], 1.0 / D, RMS_EPS,
                    ALU.mult, ALU.add), waits=[tssq])
                tv = pr.op("ve", lambda e, hp=hp: e.reciprocal(
                    tmp1.ap()[:, hp:hp + 1], tmp1.ap()[:, hp:hp + 1]), waits=[tv])
                tv = pr.op("se", lambda e, hp=hp: e.activation(
                    rstd.ap()[:, 4 + hp:5 + hp], tmp1.ap()[:, hp:hp + 1], AF.Sqrt,
                    scale=float(1.0 / D)), waits=[tv])
                T[("ssq", hp)] = tv
                rt = rope_tm(ps.ap()[:, h * 128:(h + 1) * 128], ropw.ap()[:, hp, 384:512],
                             cost[tt % 2].ap()[:, :], sint[tt % 2].ap()[:, :],
                             rstd.ap()[:, 4 + hp:5 + hp], qnb.ap()[:, :],
                             [tv, tok_cq, tok_sq, T.get(("ropw", hp))],
                             out_ta=prev_ta, bp=hp)
                th, tl = split3(ropw.ap()[:, hp, 384:512], sthi.ap()[:, hp, :],
                                stlo.ap()[:, hp, :],
                                rt + [T.get(("stkv", hp))], eng_hi="se")
                ttr = []
                for hl, srct in ((0, sthi), (1, stlo)):
                    ttr.append(pr.op("te", lambda e, hl=hl, srct=srct, hp=hp:
                               e.transpose(tbank[0].ap()[:, (2 * hp + hl) * 128:(2 * hp + hl + 1) * 128],
                                           srct.ap()[:, hp, :], ident.ap()[:, :]),
                               waits=[th, tl, T.get(("tbank", 0))]))
                teh = pr.op("ve", lambda e, h=h, tt=tt, hp=hp: e.tensor_copy(
                    q_hi.ap()[:, h, tt, :], tbank[0].ap()[:, 2 * hp * 128:(2 * hp + 1) * 128]), waits=ttr)
                tel = pr.op("ve", lambda e, h=h, tt=tt, hp=hp: e.tensor_copy(
                    q_lo.ap()[:, h, tt, :], tbank[0].ap()[:, (2 * hp + 1) * 128:(2 * hp + 2) * 128]), waits=ttr)
                T[("tbank", 0)] = tel
                T[("stkv", hp)] = tel
                T[("ropw", hp)] = tel
                last = tel
            T[("cosk", tt % 2)] = last
            T[("pswar", 4 + tt % 2)] = [last, pr.tok("se")]
            T[("xf", co)] = pr.tok("te")

        pr.phase(f"attn{g}")
        sT = [pbank[0], pbank[1]]
        pout = pbank[2]
        pl = pbank[3]
        for q2 in range(2):
            for h in range(4):
                ch = 4 * g + h
                nch = 16 * q2 + 16
                tok_out_last = None
                tok_l_last = None

                def emit_sT(c, h=h, q2=q2):
                    sb_i = c % 2
                    ps = sT[sb_i]
                    qmv_h = (q_hi.ap()[:, h, 4 * q2:4 * q2 + 4, :]
                             .rearrange("p a b -> p (a b)"))
                    qmv_l = (q_lo.ap()[:, h, 4 * q2:4 * q2 + 4, :]
                             .rearrange("p a b -> p (a b)"))
                    tokqk = mm3(ps.ap()[:, 0:512],
                                k_hi.ap()[:, c, :], k_lo.ap()[:, c, :],
                                qmv_h, qmv_l, start=True, stop=(g == 3),
                                waits=[T.get(("sT", sb_i))])
                    if g < 3:
                        bp = 32 * h
                        tp = (bp, 0)
                        tokqk = mm3(ps.ap()[:, 0:512],
                                    rks_hi.ap()[bp:bp + 32, c, :],
                                    rks_lo.ap()[bp:bp + 32, c, :],
                                    rqo_hi.ap()[bp:bp + 32, 4 * q2:4 * q2 + 4, :]
                                    .rearrange("p a b -> p (a b)"),
                                    rqo_lo.ap()[bp:bp + 32, 4 * q2:4 * q2 + 4, :]
                                    .rearrange("p a b -> p (a b)"),
                                    start=False, stop=(g != 0), waits=[], tpos=tp)
                        if g == 0:
                            tokqk = mm3(ps.ap()[:, 0:512],
                                        rqs_hi.ap()[bp:bp + 32, c, :],
                                        rqs_lo.ap()[bp:bp + 32, c, :],
                                        rko_hi.ap()[bp:bp + 32, 4 * q2:4 * q2 + 4, :]
                                        .rearrange("p a b -> p (a b)"),
                                        rko_lo.ap()[bp:bp + 32, 4 * q2:4 * q2 + 4, :]
                                        .rearrange("p a b -> p (a b)"),
                                        start=False, stop=True, waits=[], tpos=tp)
                    return tokqk

                tok_sT = {0: emit_sT(0)}
                for c in range(nch):
                    sb_i = c % 2
                    ps = sT[sb_i]
                    pbuf = c % 2
                    tokexp = pr.op("se", lambda e, ps=ps, pbuf=pbuf: e.activation(
                        p_f[pbuf].ap()[:, :], ps.ap()[:, 0:512], AF.Exp,
                        bias=shiftt.ap()[:, :]),
                        waits=[tok_sT[c], T.get(("p_f", pbuf))])
                    T[("sT", sb_i)] = tokexp
                    if c + 1 < nch:
                        tok_sT[c + 1] = emit_sT(c + 1)
                    dlt = c - 16 * q2
                    tokp = tokexp
                    if dlt >= 0:
                        tokp = pr.op("ve", lambda e, pbuf=pbuf, dlt=dlt: e.tensor_tensor(
                            p_f[pbuf].ap()[:, :], p_f[pbuf].ap()[:, :],
                            masks.ap()[:, dlt, :], ALU.mult), waits=[tokexp])
                    th, tl = split3(p_f[pbuf].ap()[:, :], p_hi[pbuf].ap()[:, :],
                                    p_lo[pbuf].ap()[:, :],
                                    [tokp, T.get(("p_s", pbuf))], eng_hi="se")
                    T[("p_f", pbuf)] = tl
                    tok_out_last = mm3(pout.ap()[:, 0:512],
                                       v_hi.ap()[:, c, :], v_lo.ap()[:, c, :],
                                       p_hi[pbuf].ap()[:, :], p_lo[pbuf].ap()[:, :],
                                       start=(c == 0), stop=(c == nch - 1),
                                       waits=[th, tl, T.get(("pout",))])
                    tok_l_last = pr.op("te", lambda e, pbuf=pbuf, c=c, nch=nch:
                                       e.matmul(
                        pl.ap()[:1, 0:512], onesf.ap()[:, :], p_f[pbuf].ap()[:, :],
                        start=(c == 0), stop=(c == nch - 1)),
                        waits=[tokp] + ([T.get(("pl",))] if c == 0 else []),
                        inc=(c == nch - 1))
                    T[("p_f", pbuf)] = [T[("p_f", pbuf)], tok_l_last]
                    T[("p_s", pbuf)] = pr.tok("te")
                tcp = pr.op("ve", lambda e: e.tensor_copy(
                    lrow.ap()[:1, :], pl.ap()[:1, 0:512]), waits=[tok_l_last])
                trc = pr.op("ve", lambda e: e.reciprocal(
                    lrec.ap()[:1, :], lrow.ap()[:1, :]), waits=[tcp])
                tbc = pr.op("te", lambda e: e.matmul(
                    pl.ap()[:, 0:512], onesr.ap()[:1, :], lrec.ap()[:1, :],
                    start=True, stop=True), waits=[trc, tcp])
                tev = pr.op("ve", lambda e: e.tensor_copy(
                    rbc.ap()[:, :], pl.ap()[:, 0:512]), waits=[tbc])
                T[("pl",)] = tev
                tsc = pr.op("ve", lambda e: e.tensor_tensor(
                    aos.ap()[:, :], pout.ap()[:, 0:512], rbc.ap()[:, :], ALU.mult),
                    waits=[tev, tok_out_last, T.get(("aosr",))])
                T[("pout",)] = tsc
                th, tl = split3(aos.ap()[:, :], aosh.ap()[:, :], aosl.ap()[:, :],
                                [tsc, T.get(("aodma",))])
                tda = []
                for sti in range(4):
                    tda.append(pr.op("sy", lambda e, ch=ch, q2=q2, sti=sti:
                               e.dma_start(
                        out=aod_hi[4 * q2 + sti, ch],
                        in_=aosh.ap()[:, 128 * sti:128 * (sti + 1)]),
                        waits=[th, tl]))
                    tda.append(pr.op("sy", lambda e, ch=ch, q2=q2, sti=sti:
                               e.dma_start(
                        out=aod_lo[4 * q2 + sti, ch],
                        in_=aosl.ap()[:, 128 * sti:128 * (sti + 1)]),
                        waits=[tl]))
                T[("aodma",)] = tda
                T[("aosr",)] = tl

    tok_ao_done = [(nm, pr.cnt[nm]) for nm in pr.pool["sy"] if pr.cnt[nm]]

    pr.phase("drain_a")
    drain_dma()
    pr.phase("wo")
    freeh("qnb", "knb", "xfh0", "xfl0",
          "wt_hi", "wt_lo", "masks", "k_hi", "k_lo", "v_hi", "v_lo",
          "rks_hi", "rks_lo", "rqs_hi", "rqs_lo", "q_hi", "q_lo",
          "rqo_hi", "rqo_lo", "rko_hi", "rko_lo",
          "p_f0", "p_f1", "p_hi0", "p_hi1", "p_lo0", "p_lo1",
          "lrow", "lrec", "rbc", "aos", "aosh", "aosl",
          "cost0", "cost1", "sint0", "sint1",
          "ropw", "ssq", "rstd", "tmp1", "sqs", "sthi", "stlo")
    dTMi = sb("dTMi", [128, NOT, NEXP], I32)
    payl = sb("payl", [128, NOT, NEXP, 2])
    lnss = sb("lnss", [128, 1])
    lnsm = sb("lnsm", [128, 1])
    lnt = sb("lnt", [128, 1])
    r1p = [sb(f"r1p{i}", [128, E]) for i in range(2)]
    aosth = [sb("aosth0", [128, NKC, 128], BF16)] * 2
    aostl = [sb("aostl0", [128, NKC, 128], BF16)] * 2
    xot = [sb(f"xot{i}", [128, 512]) for i in range(2)]
    hstage = sb("hstage", [128, E])
    hfst_hi = sb("hfst_hi", [128, NKC, 128], BF16)
    hfst_lo = sb("hfst_lo", [128, NKC, 128], BF16)
    cmbT = sb("cmbT", [8, 1024])
    flgt = sb("flgt", [8, 1024])
    post = sb("post", [8, 1024])
    d0t = post
    zzt = flgt
    dTM = sb("dTM", [128, NOT, NEXP])
    tokid = sb("tokid_s", [128, NOT])
    erowb = sb("erowb", [128, NEXP])
    initt = sb("initt", [128, 50])
    prb = sb("prb", [128, NEXP])
    prb2 = sb("prb2", [128, NEXP])
    prb3 = sb("prb3", [128, NEXP])
    cmx = sb("cmx", [128, 1])
    cm2 = sb("cm2", [128, 1])
    csum = sb("csum", [128, 1])
    cmx8 = sb("cmx8", [128, NEXP])
    identf = sb("identf", [128, 128])
    woth = sb("woth", [128, NKC, 2048], BF16)
    wotl = sb("wotl", [128, NKC, 2048], BF16)

    tin = pr.op("sy", lambda e: e.dma_start(
        out=initt.ap()[:, :],
        in_=moe_init.rearrange("e c two -> (e c two)").rearrange("(p a) -> p a", p=128)))
    tin2 = pr.op("sy", lambda e: e.dma_start(out=idcw_d, in_=initt.ap()[:, :]),
                 waits=[tin])
    tidf = pr.op("ve", lambda e: e.tensor_copy(identf.ap()[:, :], ident.ap()[:, :]))
    ttk = pr.op("sy", lambda e: e.dma_start(
        out=tokid.ap()[:, :], in_=tokid_in.rearrange("(a p) -> p a", p=128),
        allow_slow_non_contiguous=True))
    ter = pr.op("sy", lambda e: e.dma_start(
        out=erowb.ap()[:, :], in_=erow_in[None, :].broadcast_to([128, NEXP])))
    tzz = None
    T[("cmx8init",)] = pr.op("ve", lambda e: e.memset(cmx8.ap()[:, :], -1.0))

    stgv = stg.ap().rearrange("p (c o) -> p c o", o=128)
    tok_wotb = []
    for blk in range(16):
        toks = []
        for cq in range(4):
            toks.append(pr.op("sy", lambda e, cq=cq, blk=blk: e.dma_start(
                out=stgv[:, 4 * cq:4 * (cq + 1), :],
                in_=Wo[512 * cq:512 * (cq + 1), 128 * blk:128 * (blk + 1)]
                .rearrange("(c p) o -> p c o", p=128)),
                waits=[T.get(("stg",))]))
        th, tl = split3(stgv[:, :, :],
                        woth.ap()[:, :, 128 * blk:128 * (blk + 1)],
                        wotl.ap()[:, :, 128 * blk:128 * (blk + 1)],
                        toks + [T.get(("wts",))], eng_hi="se")
        T[("stg",)] = tl
        T[("wts",)] = tl
        tok_wotb.append(tl)

    def post_st(st):
        bi = st % 2
        w8 = T[("r1d", st)]
        tq = pr.op("se", lambda e, bi=bi: e.activation(
            hstage.ap()[:, :], r1p[bi].ap()[:, :], AF.Square,
            accum_out=lnss.ap()[:, 0:1]),
            waits=w8 + [T.get(("lnx",)), T.get(("hstg",))])
        tsm = pr.op("ve", lambda e, bi=bi: e.reduce_sum(
            lnsm.ap()[:, 0:1], r1p[bi].ap()[:, :], axis=AX.X),
            waits=w8 + [T.get(("lnx",))])
        tmu = pr.op("ve", lambda e, st=st: e.tensor_scalar(
            mu1.ap()[:, st:st + 1], lnsm.ap()[:, 0:1], 1.0 / E, None, ALU.mult),
            waits=[tsm])
        tms = pr.op("ve", lambda e: e.tensor_scalar(
            lnss.ap()[:, 0:1], lnss.ap()[:, 0:1], 1.0 / E, LN_EPS, ALU.mult, ALU.add),
            waits=[tq])
        tm2 = pr.op("ve", lambda e, st=st: e.tensor_tensor(
            lnt.ap()[:, 0:1], mu1.ap()[:, st:st + 1], mu1.ap()[:, st:st + 1],
            ALU.mult), waits=[tmu])
        tvr = pr.op("ve", lambda e: e.tensor_tensor(
            lnt.ap()[:, 0:1], lnss.ap()[:, 0:1], lnt.ap()[:, 0:1], ALU.subtract),
            waits=[tms, tm2])
        trc = pr.op("ve", lambda e: e.reciprocal(lnt.ap()[:, 0:1], lnt.ap()[:, 0:1]),
                    waits=[tvr])
        tsq = pr.op("se", lambda e, st=st: e.activation(
            s1v.ap()[:, st:st + 1], lnt.ap()[:, 0:1], AF.Sqrt), waits=[trc])
        T[("lnx",)] = tsq
        th2 = pr.op("ve", lambda e, st=st, bi=bi: e.tensor_scalar(
            hstage.ap()[:, :], r1p[bi].ap()[:, :], mu1.ap()[:, st:st + 1],
            s1v.ap()[:, st:st + 1], ALU.subtract, ALU.mult),
            waits=w8 + [tsq, T.get(("hstg",))])
        T[("r1w", bi)] = [tq, tsm, th2]
        td = pr.op("sy", lambda e, st=st: e.dma_start(
            out=h2nd[st * 128:(st + 1) * 128, :], in_=hstage.ap()[:, :]),
            waits=[th2])
        sh2, sl2 = split3(hstage.ap()[:, :], shi.ap()[:, :], slo.ap()[:, :],
                          [th2, T.get(("shilo",))])
        T[("hstg",)] = [td, sl2]

        def spec_h(half, hl):
            dt_ = hfst_hi if hl == 0 else hfst_lo
            return dt_.ap()[:, half * 8:(half + 1) * 8, :]
        tevh = transpose16(shi, slo, spec_h, [sh2, sl2, T.get(("hfstw",))])
        T[("shilo",)] = tevh
        tokm = None
        for c in range(NKC):
            tokm = mm3(pbank[5].ap()[:, 0:NEXP],
                       hfst_hi.ap()[:, c, :], hfst_lo.ap()[:, c, :],
                       gwhi.ap()[:, c, :], gwlo.ap()[:, c, :],
                       start=(c == 0), stop=(c == NKC - 1),
                       waits=[tevh] + T["consts"] + [T.get(("gps",))])
        T[("hfstw",)] = tokm
        tmx = pr.op("ve", lambda e: e.reduce_max(
            cmx.ap()[:, :], pbank[5].ap()[:, 0:NEXP], axis=AX.X), waits=[tokm])
        tng = pr.op("ve", lambda e: e.tensor_scalar(
            cm2.ap()[:, :], cmx.ap()[:, :], -1.0, None, ALU.mult), waits=[tmx])
        tex = pr.op("se", lambda e: e.activation(
            prb.ap()[:, :], pbank[5].ap()[:, 0:NEXP], AF.Exp, bias=cm2.ap()[:, :]),
            waits=[tng, T.get(("prb",))])
        T[("gps",)] = tex
        tsm2 = pr.op("ve", lambda e: e.reduce_sum(
            csum.ap()[:, :], prb.ap()[:, :], axis=AX.X), waits=[tex])
        trc2 = pr.op("ve", lambda e: e.reciprocal(
            csum.ap()[:, :], csum.ap()[:, :]), waits=[tsm2])
        tpr = pr.op("ve", lambda e: e.tensor_scalar(
            prb.ap()[:, :], prb.ap()[:, :], csum.ap()[:, :], None, ALU.mult),
            waits=[trc2, tex])
        tm8 = pr.op("ve", lambda e: e.max(
            prb2.ap()[:, :], prb.ap()[:, :]), waits=[tpr, T.get(("prb2",))])
        tcp8 = pr.op("ve", lambda e: e.tensor_copy(
            cmx8.ap()[:, 0:2], prb2.ap()[:, 0:2]),
            waits=[tm8, T[("cmx8init",)], T.get(("cmx8",))])
        tmr = pr.op("ve", lambda e: e.match_replace(
            prb3.ap()[:, :], cmx8.ap()[:, :], prb.ap()[:, :], 0.0),
            waits=[tcp8, tpr])
        T[("cmx8",)] = tmr
        tden = pr.op("ve", lambda e: e.tensor_tensor(
            csum.ap()[:, :], prb2.ap()[:, 0:1], prb2.ap()[:, 1:2], ALU.add),
            waits=[tm8, trc2])
        trd = pr.op("ve", lambda e: e.reciprocal(
            csum.ap()[:, :], csum.ap()[:, :]), waits=[tden])
        tdf = pr.op("ve", lambda e: e.tensor_tensor(
            prb2.ap()[:, :], prb.ap()[:, :], prb3.ap()[:, :], ALU.subtract),
            waits=[tmr, tpr])
        T[("prb2",)] = tdf
        tcb = pr.op("ve", lambda e, st=st: e.tensor_scalar(
            combine.ap()[:, st, :], prb2.ap()[:, :], csum.ap()[:, :], None,
            ALU.mult), waits=[trd, tdf])
        T[("prb",)] = tcb
        T[("cmb", st)] = tcb

    for st in range(NOT):
        bi = st % 2
        tld = []
        for cq in range(4):
            tld.append(pr.op("sy", lambda e, st=st, cq=cq, bi=bi: e.dma_start(
                out=aosth[bi].ap()[:, 4 * cq:4 * (cq + 1), :],
                in_=aod_hi[st, 4 * cq:4 * (cq + 1)]
                .rearrange("c d k -> d c k")),
                waits=[tok_ao_done, T.get(("aost", 0))]))
            tld.append(pr.op("sy", lambda e, st=st, cq=cq, bi=bi: e.dma_start(
                out=aostl[bi].ap()[:, 4 * cq:4 * (cq + 1), :],
                in_=aod_lo[st, 4 * cq:4 * (cq + 1)]
                .rearrange("c d k -> d c k")),
                waits=[tok_ao_done, T.get(("aost", 0))]))
        tr1s = []
        tokm = None
        for ob in range(4):
            ps = pbank[ob % 2]
            tok_x = pr.op("sy", lambda e, st=st, ob=ob: e.dma_start(
                out=xot[ob % 2].ap()[:, :],
                in_=x_own[st * 128:(st + 1) * 128, 512 * ob:512 * (ob + 1)]),
                waits=[T.get(("xot", ob % 2))])
            for c in range(NKC):
                tokm = mm3(ps.ap()[:, 0:512],
                           aosth[bi].ap()[:, c, :], aostl[bi].ap()[:, c, :],
                           woth.ap()[:, c, 512 * ob:512 * (ob + 1)],
                           wotl.ap()[:, c, 512 * ob:512 * (ob + 1)],
                           start=(c == 0), stop=(c == NKC - 1),
                           waits=tld + [tok_wotb[4 * ob + 3],
                                        T.get(("wops", ob % 2))])
            tr1 = pr.op("ve", lambda e, ps=ps, ob=ob, bi=bi: e.tensor_tensor(
                r1p[bi].ap()[:, ob * 512:(ob + 1) * 512], ps.ap()[:, 0:512],
                xot[ob % 2].ap()[:, :], ALU.add),
                waits=[tokm, tok_x, T.get(("r1w", bi))])
            T[("wops", ob % 2)] = tr1
            T[("xot", ob % 2)] = tr1
            tr1s.append(tr1)
        T[("aost", 0)] = tokm
        T[("r1d", st)] = tr1s
        if st > 0:
            post_st(st - 1)
    post_st(NOT - 1)
    cmb_done = [T[("cmb", st)] for st in range(NOT)]

    ttr = []
    for st in range(NOT):
        pb = pbank[3 + st // 4]
        ttr.append(pr.op("te", lambda e, st=st, pb=pb: e.transpose(
            pb.ap()[0:8, (st % 4) * 128:(st % 4 + 1) * 128],
            combine.ap()[:, st, :], identf.ap()[:, :]),
            waits=cmb_done + [tidf]))
    tev1 = pr.op("ve", lambda e: e.tensor_copy(
        cmbT.ap()[:, 0:512], pbank[3].ap()[0:8, 0:512]), waits=ttr)
    tev2 = pr.op("ve", lambda e: e.tensor_copy(
        cmbT.ap()[:, 512:1024], pbank[4].ap()[0:8, 0:512]), waits=ttr)
    tsg = pr.op("se", lambda e: e.sign(flgt.ap()[:, :], cmbT.ap()[:, :]),
                waits=[tev1, tev2])
    tscn = pr.op("ve", lambda e: e.tensor_tensor_scan(
        post.ap()[:, :], flgt.ap()[:, :], zzt.ap()[:, :], 0.0, ALU.add, ALU.bypass),
        waits=[tsg, tzz])
    tex1 = pr.op("ve", lambda e: e.tensor_tensor(
        post.ap()[:, :], post.ap()[:, :], flgt.ap()[:, :], ALU.subtract),
        waits=[tscn])
    td0 = pr.op("ve", lambda e: e.tensor_scalar(
        d0t.ap()[:, :], post.ap()[:, :], -BIGV, None, ALU.add), waits=[tex1])
    td0 = pr.op("ve", lambda e: e.tensor_tensor(
        d0t.ap()[:, :], d0t.ap()[:, :], flgt.ap()[:, :], ALU.mult), waits=[td0])
    td0 = pr.op("ve", lambda e: e.tensor_scalar(
        d0t.ap()[:, :], d0t.ap()[:, :], BIGV, None, ALU.add), waits=[td0])
    ttd = []
    for st in range(NOT):
        ttd.append(pr.op("te", lambda e, st=st: e.transpose(
            pbank[5].ap()[:, 64 + st * NEXP:64 + (st + 1) * NEXP],
            d0t.ap()[:, st * 128:(st + 1) * 128], identf.ap()[0:8, 0:NEXP]),
            waits=[td0, tev1, tev2]))
    tevd = pr.op("ve", lambda e: e.tensor_copy(
        dTM.ap()[:, :, :], pbank[5].ap()[:, 64:64 + NOT * NEXP]), waits=ttd)
    tprev = tevd
    for st in range(NOT):
        tprev = pr.op("ve", lambda e, st=st: e.tensor_scalar(
            dTM.ap()[:, st, :], dTM.ap()[:, st, :], float(CAP + st), None,
            ALU.min), waits=[tprev])
    for st in range(NOT):
        tprev = pr.op("ve", lambda e, st=st: e.tensor_tensor(
            dTM.ap()[:, st, :], dTM.ap()[:, st, :], erowb.ap()[:, :], ALU.add),
            waits=[tprev, ter])
    tcast = pr.op("ve", lambda e: e.tensor_copy(
        dTMi.ap()[:, :, :], dTM.ap()[:, :, :]), waits=[tprev])
    tpl = []
    for st in range(NOT):
        tpl.append(pr.op("ve", lambda e, st=st: e.tensor_copy(
            payl.ap()[:, st, :, 0],
            tokid.ap()[:, st:st + 1].broadcast_to([128, NEXP])), waits=[ttk]))
        tpl.append(pr.op("ve", lambda e, st=st: e.tensor_copy(
            payl.ap()[:, st, :, 1], combine.ap()[:, st, :]), waits=cmb_done))
    first_w = [tin2, tcast] + tpl
    esc = {}

    def emit_disp(e_):
        toks = []
        for st in range(NOT):
            toks.append(pr.op("gp", lambda e, e_=e_, st=st: e.indirect_dma_start(
                out=idcw_d.rearrange("e c two -> (e c) two"),
                out_offset=IndirectOffsetOnAxis(dTMi.ap()[:, st, e_:e_ + 1], 0),
                in_=payl.ap()[:, st, e_, :], in_offset=None), waits=first_w))
        esc[e_] = toks


    pr.phase("drain_b")
    drain_dma()
    pr.phase("ffn_alloc")
    freeh("woth", "wotl", "aosth0", "aostl0", "xot0", "xot1",
          "hstage", "hfst_hi", "hfst_lo", "r1p0", "r1p1",
          "cmbT", "flgt", "post", "dTM",
          "tokid_s", "erowb", "initt", "prb", "prb2", "prb3", "cmx", "cm2",
          "csum", "cmx8", "identf")
    idcw2 = sb("idcw2", [128, 2 * NSLT, 2])
    idxi = sb("idxi", [128, 2 * NSLT], I32)
    gtile = [sb(f"gtile{i}", [128, E]) for i in range(2)]
    hfme_hi = [sb(f"hfme_hi{i}", [128, NKC, CAP], BF16) for i in range(2)]
    hfme_lo = [sb(f"hfme_lo{i}", [128, NKC, CAP], BF16) for i in range(2)]
    afh = sb("afh", [128, FCH, CAP], BF16)
    afl = sb("afl", [128, FCH, CAP], BF16)
    yslot = sb("yslot", [128, NSLT, E])
    wgf = [sb(f"wgf{i}", [128, NKC, 128]) for i in range(2)]
    wgh = [sb(f"wgh{i}", [128, NKC, 128], BF16) for i in range(2)]
    wgl = [sb(f"wgl{i}", [128, NKC, 128], BF16) for i in range(2)]
    wuf = [sb(f"wuf{i}", [128, NKC, 128]) for i in range(2)]
    wuh = [sb(f"wuh{i}", [128, NKC, 128], BF16) for i in range(2)]
    wul = [sb(f"wul{i}", [128, NKC, 128], BF16) for i in range(2)]
    wdf = [sb(f"wdf{i}", [128, 1024]) for i in range(2)]
    wdh = [sb(f"wdh{i}", [128, 1024], BF16) for i in range(2)]
    wdl = [sb(f"wdl{i}", [128, 1024], BF16) for i in range(2)]
    silu_t = sb("silu_t", [128, CAP])
    a_t = silu_t

    pr.phase("ffn")
    tmz = pr.op("ve", lambda e: e.memset(stg.ap()[:, :], 0.0))
    zt = []
    zth = []
    for st in range(NOT):
        zt.append(pr.op("sy", lambda e, st=st: e.dma_start(
            out=y_d[st * 128:(st + 1) * 128, :], in_=stg.ap()[:, :]), waits=[tmz]))
    for dr in range(NSLT):
        zt.append(pr.op("sy", lambda e, dr=dr: e.dma_start(
            out=y_d[1024 + dr:1025 + dr, :], in_=stg.ap()[0:1, :]), waits=[tmz]))
        zth.append(pr.op("sy", lambda e, dr=dr: e.dma_start(
            out=h2nd[1024 + dr:1025 + dr, :], in_=stg.ap()[0:1, :]), waits=[tmz]))
    zt = zt + zth
    def emit_gather(e_):
        ep = e_ % 2
        for t3 in range(NSLT):
            sl = ep * NSLT + t3
            til = pr.op("gp", lambda e, e_=e_, t3=t3, sl=sl: e.dma_start(
                out=idcw2.ap()[:, sl, :],
                in_=idcw_d[e_, t3 * 128:(t3 + 1) * 128, :]),
                waits=esc[e_] + first_w +
                [T.get(("idxfw", sl)), T.get(("cwfw", sl))])
            T[("cwld", sl)] = til
            tci = pr.op("ve", lambda e, sl=sl: e.tensor_copy(
                idxi.ap()[:, sl:sl + 1], idcw2.ap()[:, sl, 0:1]),
                waits=[til, T.get(("idxiw", sl))])
            T[("idxfw", sl)] = tci
            tg = pr.op("gp", lambda e, sl=sl, t3=t3: e.indirect_dma_start(
                out=gtile[t3 % 2].ap()[:, :], out_offset=None,
                in_=h2nd, in_offset=IndirectOffsetOnAxis(idxi.ap()[:, sl:sl + 1], 0)),
                waits=[tci] + zth + [T.get(("gtw", t3 % 2))])
            th, tl = split3(gtile[t3 % 2].ap()[:, :], shi.ap()[:, :], slo.ap()[:, :],
                            [tg, T.get(("shilo",))])
            T[("gtw", t3 % 2)] = tl

            def spec_g(half, hl, t3=t3, ep=ep):
                dt_ = hfme_hi[ep] if hl == 0 else hfme_lo[ep]
                return dt_.ap()[:, half * 8:(half + 1) * 8, t3 * 128:(t3 + 1) * 128]
            tev = transpose16(shi, slo, spec_g, [th, tl, T.get(("hfmew", ep))])
            T[("shilo",)] = tev
            T[("hfld", e_, t3)] = tev

    emit_disp(0)
    emit_gather(0)
    for e_ in range(NEXP):
        ep = e_ % 2
        hfme_done = [T[("hfld", e_, t3)] for t3 in range(NSLT)]
        for fc in range(FCH):
            bi = fc % 2
            wg_toks = []
            for cq in range(4):
                wg_toks.append(pr.op("sy", lambda e, e_=e_, fc=fc, cq=cq, bi=bi:
                               e.dma_start(
                    out=wgf[bi].ap()[:, 4 * cq:4 * (cq + 1), :],
                    in_=w_gate[e_, 512 * cq:512 * (cq + 1), fc * 128:(fc + 1) * 128]
                    .rearrange("(c p) f -> p c f", p=128)),
                    waits=[T.get(("wgfw", bi))]))
            th = pr.op("se", lambda e, bi=bi: e.copy(
                wgh[bi].ap()[:, :, :], wgf[bi].ap()[:, :, :]),
                waits=wg_toks + [T.get(("wgs", bi))])
            tl = pr.op("ve", lambda e, bi=bi: e.tensor_tensor(
                wgl[bi].ap()[:, :, :], wgf[bi].ap()[:, :, :], wgh[bi].ap()[:, :, :],
                ALU.subtract), waits=wg_toks + [th])
            wg_done = tl
            T[("wgfw", bi)] = tl
            wu_toks = []
            for cq in range(4):
                wu_toks.append(pr.op("sy", lambda e, e_=e_, fc=fc, cq=cq, bi=bi:
                               e.dma_start(
                    out=wuf[bi].ap()[:, 4 * cq:4 * (cq + 1), :],
                    in_=w_up[e_, 512 * cq:512 * (cq + 1), fc * 128:(fc + 1) * 128]
                    .rearrange("(c p) f -> p c f", p=128)),
                    waits=[T.get(("wufw", bi))]))
            th2 = pr.op("se", lambda e, bi=bi: e.copy(
                wuh[bi].ap()[:, :, :], wuf[bi].ap()[:, :, :]),
                waits=wu_toks + [T.get(("wus", bi))])
            tl2 = pr.op("ve", lambda e, bi=bi: e.tensor_tensor(
                wul[bi].ap()[:, :, :], wuf[bi].ap()[:, :, :], wuh[bi].ap()[:, :, :],
                ALU.subtract), waits=wu_toks + [th2])
            T[("wufw", bi)] = tl2
            T[("wgsplit",)] = th2
            tokg = toku = None
            for c in range(NKC):
                tokg = mm3(pbank[0].ap()[:, 0:CAP],
                           wgh[bi].ap()[:, c, :], wgl[bi].ap()[:, c, :],
                           hfme_hi[ep].ap()[:, c, :], hfme_lo[ep].ap()[:, c, :],
                           start=(c == 0), stop=(c == NKC - 1),
                           waits=[wg_done, th] + hfme_done + [T.get(("gu0",))])
            for c in range(NKC):
                toku = mm3(pbank[1].ap()[:, 0:CAP],
                           wuh[bi].ap()[:, c, :], wul[bi].ap()[:, c, :],
                           hfme_hi[ep].ap()[:, c, :], hfme_lo[ep].ap()[:, c, :],
                           start=(c == 0), stop=(c == NKC - 1),
                           waits=[tl2, th2, T.get(("gu1",))])
            T[("wgs", bi)] = tokg
            T[("wus", bi)] = toku
            T[("hfmew", ep)] = toku
            tsl = pr.op("se", lambda e: e.activation(
                silu_t.ap()[:, :], pbank[0].ap()[:, 0:CAP], AF.Silu),
                waits=[tokg, T.get(("silu",)), T.get(("a_t",))])
            T[("gu0",)] = tsl
            tmu_ = pr.op("ve", lambda e: e.tensor_tensor(
                a_t.ap()[:, :], silu_t.ap()[:, :], pbank[1].ap()[:, 0:CAP],
                ALU.mult), waits=[tsl, toku])
            T[("gu1",)] = tmu_
            T[("silu",)] = tmu_
            th3, tl3 = split3(a_t.ap()[:, :], afh.ap()[:, fc, :],
                              afl.ap()[:, fc, :], [tmu_, T.get(("af", fc))])
            T[("a_t",)] = tl3
            T[("af", fc)] = tl3
        a_done = [T[("af", fc)] for fc in range(FCH)]
        evac_toks = {t3: [] for t3 in range(NSLT)}
        for obp in range(2):
            tokm_d = {}
            last_mm = None
            for fc in range(FCH):
                bi = fc % 2
                tok_wd = pr.op("sy", lambda e, e_=e_, fc=fc, obp=obp, bi=bi:
                               e.dma_start(
                    out=wdf[bi].ap()[:, :],
                    in_=w_down[e_, fc * 128:(fc + 1) * 128,
                               obp * 1024:(obp + 1) * 1024]),
                    waits=[T.get(("wdfw", bi))])
                th = pr.op("se", lambda e, bi=bi: e.copy(
                    wdh[bi].ap()[:, :], wdf[bi].ap()[:, :]),
                    waits=[tok_wd, T.get(("wds", bi))])
                tl = pr.op("ve", lambda e, bi=bi: e.tensor_tensor(
                    wdl[bi].ap()[:, :], wdf[bi].ap()[:, :], wdh[bi].ap()[:, :],
                    ALU.subtract), waits=[tok_wd, th])
                T[("wdfw", bi)] = tl
                for t3 in range(NSLT):
                    for ob in range(2):
                        pi = t3 * 2 + ob
                        last_mm = mm3(pbank[pi].ap()[:, 0:512],
                                      afh.ap()[:, fc, t3 * 128:(t3 + 1) * 128],
                                      afl.ap()[:, fc, t3 * 128:(t3 + 1) * 128],
                                      wdh[bi].ap()[:, ob * 512:(ob + 1) * 512],
                                      wdl[bi].ap()[:, ob * 512:(ob + 1) * 512],
                                      start=(fc == 0), stop=(fc == FCH - 1),
                                      waits=a_done + [tl, th] +
                                      ([T.get(("dps", pi))] if fc == 0 else []),
                                      force_tok=(t3 == NSLT - 1 and ob == 1))
                        if fc == FCH - 1:
                            tokm_d[pi] = last_mm
                T[("wds", bi)] = last_mm
            for t3 in range(NSLT):
                for ob in range(2):
                    pi = t3 * 2 + ob
                    dst = yslot.ap()[:, t3, (obp * 1024 + ob * 512):
                                     (obp * 1024 + (ob + 1) * 512)]
                    tev = pr.op("ve", lambda e, pi=pi, dst=dst, t3=t3, ep=ep:
                                e.tensor_scalar(
                                    dst, pbank[pi].ap()[:, 0:512],
                                    idcw2.ap()[:, ep * NSLT + t3, 1:2],
                                    None, ALU.mult),
                                waits=[tokm_d[pi], T.get(("cwld", ep * NSLT + t3)),
                                       T.get(("ysl", t3))])
                    T[("dps", pi)] = tev
                    evac_toks[t3].append(tev)
        if e_ + 1 < NEXP:
            emit_disp(e_ + 1)
            emit_gather(e_ + 1)
        escs = []
        for t3 in range(NSLT):
            sl = ep * NSLT + t3
            ysc = pr.op("gp", lambda e, sl=sl, t3=t3: e.indirect_dma_start(
                out=y_d, out_offset=IndirectOffsetOnAxis(idxi.ap()[:, sl:sl + 1], 0),
                in_=yslot.ap()[:, t3, :], in_offset=None, compute_op=ALU.add),
                waits=evac_toks[t3] + zt + T.get(("ysc",), []))
            escs.append(ysc)
            T[("ysl", t3)] = ysc
            T[("idxiw", sl)] = [ysc]
            T[("cwfw", sl)] = evac_toks[t3][-1]
        T[("ysc",)] = escs

    pr.phase("drain_c")
    drain_dma()
    pr.phase("ln2")
    h2t = [wgf[0].ap().rearrange("p c k -> p (c k)"),
           wgf[1].ap().rearrange("p c k -> p (c k)")]
    for st in range(NOT):
        bi = st % 2
        ty = pr.op("sy", lambda e, st=st, bi=bi: e.dma_start(
            out=gtile[bi].ap()[:, :], in_=y_d[st * 128:(st + 1) * 128, :]),
            waits=[T.get(("l2y", bi))])
        th2 = pr.op("sy", lambda e, st=st, bi=bi: e.dma_start(
            out=h2t[bi][:, :], in_=h2nd[st * 128:(st + 1) * 128, :]),
            waits=[T.get(("l2h", bi))])
        tr2 = pr.op("ve", lambda e, bi=bi: e.tensor_tensor(
            gtile[bi].ap()[:, :], gtile[bi].ap()[:, :], h2t[bi][:, :],
            ALU.add), waits=[ty, th2])
        tq = pr.op("se", lambda e, bi=bi: e.activation(
            h2t[bi][:, :], gtile[bi].ap()[:, :], AF.Square,
            accum_out=lnss.ap()[:, 0:1]), waits=[tr2, T.get(("lnx",))])
        T[("l2h", bi)] = tq
        tsm = pr.op("ve", lambda e, bi=bi: e.reduce_sum(
            lnsm.ap()[:, 0:1], gtile[bi].ap()[:, :], axis=AX.X),
            waits=[tr2, T.get(("lnx",))])
        tmu_ = pr.op("ve", lambda e: e.tensor_scalar(
            lnsm.ap()[:, 0:1], lnsm.ap()[:, 0:1], 1.0 / E, None, ALU.mult),
            waits=[tsm])
        tms = pr.op("ve", lambda e: e.tensor_scalar(
            lnss.ap()[:, 0:1], lnss.ap()[:, 0:1], 1.0 / E, LN_EPS,
            ALU.mult, ALU.add), waits=[tq])
        tm2 = pr.op("ve", lambda e: e.tensor_tensor(
            lnt.ap()[:, 0:1], lnsm.ap()[:, 0:1], lnsm.ap()[:, 0:1], ALU.mult),
            waits=[tmu_])
        tvr = pr.op("ve", lambda e: e.tensor_tensor(
            lnt.ap()[:, 0:1], lnss.ap()[:, 0:1], lnt.ap()[:, 0:1],
            ALU.subtract), waits=[tms, tm2])
        trc = pr.op("ve", lambda e: e.reciprocal(
            lnt.ap()[:, 0:1], lnt.ap()[:, 0:1]), waits=[tvr])
        tsq = pr.op("se", lambda e: e.activation(
            lnt.ap()[:, 0:1], lnt.ap()[:, 0:1], AF.Sqrt), waits=[trc])
        T[("lnx",)] = tsq
        tou = pr.op("ve", lambda e, bi=bi: e.tensor_scalar(
            stg.ap()[:, :], gtile[bi].ap()[:, :], lnsm.ap()[:, 0:1],
            lnt.ap()[:, 0:1], ALU.subtract, ALU.mult),
            waits=[tsq, tr2, T.get(("ostg",))])
        tst = pr.op("sy", lambda e, st=st: e.dma_start(
            out=out_own[st * 128:(st + 1) * 128, :], in_=stg.ap()[:, :]),
            waits=[tou])
        T[("ostg",)] = tst
        T[("l2y", bi)] = tou
        T[("store", st)] = tst

    pr.phase("fin")
    pr.op("sy", lambda e: e.nop(),
          waits=[T[("store", st)] for st in range(NOT)], inc=False)

    pr.emit()
    return nc


_NC_CACHE = {}


def _get_nc():
    if "nc" not in _NC_CACHE:
        _NC_CACHE["nc"] = build_program()
    return _NC_CACHE["nc"]


def make_in_maps(x, common):
    in_maps = []
    for c in range(8):
        b, j = c // 4, c % 4
        cos_k, sin_k, cos_q, sin_q, masks = _host_tables(j)
        m = dict(common)
        m["x_own"] = np.ascontiguousarray(x[b, j::4, :])
        m["x_keys"] = np.ascontiguousarray(x[b])
        m["masks_in"] = masks
        m["cos_k"] = cos_k
        m["sin_k"] = sin_k
        m["cos_q"] = cos_q
        m["sin_q"] = sin_q
        in_maps.append(m)
    return in_maps


def _moe_init():
    init = np.zeros((NEXP, CAP + 16, 2), np.float32)
    init[:, :, 0] = DUMPT
    for t3 in range(NSLT):
        init[:, t3 * 128:(t3 + 1) * 128, 0] = DUMPT + t3
    return init


def make_common(L0_attn_Wq, L0_attn_Wk, L0_attn_Wv, L0_attn_Wo,
                L0_attn_q_norm_w, L0_attn_k_norm_w, L0_attn_Wrq, L0_attn_Wrk,
                L0_attn_graph_scale, L0_ffn_gate_w, L0_ffn_w_gate, L0_ffn_w_up,
                L0_ffn_w_down):
    cc = np.ascontiguousarray
    return {
        "Wq": cc(np.asarray(L0_attn_Wq, np.float32).T),
        "Wk": cc(np.asarray(L0_attn_Wk, np.float32).T),
        "Wv": cc(np.asarray(L0_attn_Wv, np.float32).T),
        "Wo": cc(np.asarray(L0_attn_Wo, np.float32).T),
        "Wrq": cc(np.asarray(L0_attn_Wrq, np.float32).T),
        "Wrk": cc(np.asarray(L0_attn_Wrk, np.float32).T),
        "qnw": cc(np.asarray(L0_attn_q_norm_w, np.float32)),
        "knw": cc(np.asarray(L0_attn_k_norm_w, np.float32)),
        "gsc": cc(np.asarray(L0_attn_graph_scale, np.float32)),
        "gate_w": cc(np.asarray(L0_ffn_gate_w, np.float32)),
        "w_gate": cc(np.asarray(L0_ffn_w_gate, np.float32)),
        "w_up": cc(np.asarray(L0_ffn_w_up, np.float32)),
        "w_down": cc(np.asarray(L0_ffn_w_down, np.float32)),
        "ideye": np.eye(128, dtype=BF),
        "moe_init": _moe_init(),
        "erow": (np.arange(NEXP) * (CAP + 16)).astype(np.float32),
        "tokid_c": np.arange(1024, dtype=np.float32),
    }


def kernel(L0_attn_x, L0_attn_Wq, L0_attn_Wk, L0_attn_Wv, L0_attn_Wo,
           L0_attn_q_norm_w, L0_attn_k_norm_w, L0_attn_Wrq, L0_attn_Wrk,
           L0_attn_graph_scale, L0_ffn_gate_w, L0_ffn_w_gate, L0_ffn_w_up,
           L0_ffn_w_down):
    x = np.ascontiguousarray(np.asarray(L0_attn_x, dtype=np.float32))
    common = make_common(L0_attn_Wq, L0_attn_Wk, L0_attn_Wv, L0_attn_Wo,
                         L0_attn_q_norm_w, L0_attn_k_norm_w, L0_attn_Wrq,
                         L0_attn_Wrk, L0_attn_graph_scale, L0_ffn_gate_w,
                         L0_ffn_w_gate, L0_ffn_w_up, L0_ffn_w_down)
    in_maps = make_in_maps(x, common)
    nc = _get_nc()
    res = run_bass_kernel_spmd(nc, in_maps, core_ids=list(range(8)))
    out = np.empty((B, S, E), np.float32)
    for c in range(8):
        b, j = c // 4, c % 4
        out[b, j::4, :] = res.results[c]["out_own"]
    return out
